# version 03
# baseline (speedup 1.0000x reference)
import numpy as np
import ml_dtypes
import concourse.bass as bass
import concourse.mybir as mybir
from concourse.bass_utils import run_bass_kernel_spmd

F32 = mybir.dt.float32
BF16 = mybir.dt.bfloat16
AF = mybir.ActivationFunctionType
ALU = mybir.AluOpType
BF = ml_dtypes.bfloat16
AX = mybir.AxisListType.X

NT = 1024
DM = 2048
NH = 16
HD = 5632
FC = DM // 128
TC = NT // 128
NBLK = 4
HC = HD // 128
NW_ELEMS = float(HD * DM)

ENGINES = ("sync", "scalar", "vector", "tensor", "gpsimd")


class Sem:
    __slots__ = ("h", "v", "name")

    def __init__(self, h, name):
        self.h, self.v, self.name = h, 0, name


class Op:
    __slots__ = ("meth", "args", "kwargs", "incs")

    def __init__(self, meth, args, kwargs):
        self.meth, self.args, self.kwargs, self.incs = meth, args, kwargs, []

    def inc(self, sem, n):
        assert not self.incs, f"op {self.meth} already has a sync update"
        self.incs.append((sem, n, sem.v + n))
        sem.v += n
        return self

    def tok(self):
        if self.incs:
            s, n, va = self.incs[0]
            return (s, va)
        return None


class Rec:
    def __init__(self, nc):
        self.nc = nc
        self.q = {e: [] for e in ENGINES}
        self._n = 0
        self._lastwait = {}

    def sem(self, name=None):
        self._n += 1
        nm = name or f"s{self._n}"
        return Sem(self.nc.alloc_semaphore(nm), nm)

    def op(self, eng, meth, *args, **kwargs):
        o = Op(meth, args, kwargs)
        self.q[eng].append(o)
        return o

    def wait(self, eng, sem, val=None):
        v = sem.v if val is None else val
        if v <= 0:
            return
        key = (eng, sem.name)
        if self._lastwait.get(key, -1) >= v:
            return
        self._lastwait[key] = v
        self.q[eng].append(Op("wait_ge", (sem.h, v), {}))

    def dma(self, eng, out, in_, sem, **kw):
        return self.op(eng, "dma_start", out=out, in_=in_, **kw).inc(sem, 16)

    @staticmethod
    def _interval(ap):
        ext = 1
        for (step, count) in ap.ap:
            ext += abs(int(step)) * (int(count) - 1)
        off = int(ap.offset)
        return (ap.tensor.name, off, off + ext)

    def _rw_of(self, o):
        if o.meth in ("wait_ge", "nop"):
            return [], []
        writes, reads = [], []
        args = list(o.args)
        kw = dict(o.kwargs)
        if o.meth == "memset":
            writes.append(self._interval(args[0]))
            return writes, reads
        if "out" in kw:
            writes.append(self._interval(kw.pop("out")))
        for v in list(args) + list(kw.values()):
            if isinstance(v, bass.AP):
                reads.append(self._interval(v))
        return writes, reads

    def _auto_chain(self):
        def overlap(a, b):
            return a[0] == b[0] and a[1] < b[2] and b[1] < a[2]

        for eng in ("vector", "scalar", "gpsimd"):
            ops = self.q[eng]
            chain = self.sem(f"chain_{eng}")
            writers = []
            readers = []
            n = len(ops)
            deps = [None] * n
            is_dma = [o.meth == "dma_start" for o in ops]
            for i, o in enumerate(ops):
                ws, rs = self._rw_of(o)
                need = set()
                for r in rs:
                    for (w, j) in writers:
                        if (not is_dma[j]) and overlap(r, w):
                            need.add(j)
                for w in ws:
                    for (pw, j) in writers:
                        if (not is_dma[j]) and overlap(w, pw):
                            need.add(j)
                    for (pr, j) in readers:
                        if (not is_dma[j]) and overlap(w, pr):
                            need.add(j)
                deps[i] = need
                for w in ws:
                    writers.append((w, i))
                for r in rs:
                    readers.append((r, i))
            producers = set()
            for need in deps:
                producers |= need
            chain_val = {}
            cv = 0
            for j in sorted(producers):
                if not ops[j].incs:
                    cv += 1
                    ops[j].inc(chain, 1)
                    chain_val[j] = cv
                else:
                    pass
            new_ops = []
            lastw = {}
            for i, o in enumerate(ops):
                if deps[i]:
                    for j in sorted(deps[i]):
                        t = ops[j].tok()
                        s, v = t
                        if lastw.get(s.name, -1) < v:
                            new_ops.append(Op("wait_ge", (s.h, v), {}))
                            lastw[s.name] = v
                new_ops.append(o)
            self.q[eng] = new_ops

    def replay(self):
        self._auto_chain()
        with self.nc.Block() as block:
            for eng in ENGINES:
                ops = self.q[eng]

                def runner(engine, ops=ops):
                    for o in ops:
                        r = getattr(engine, o.meth)(*o.args, **o.kwargs)
                        if o.incs:
                            s, nn_, _ = o.incs[0]
                            r.then_inc(s.h, nn_)

                getattr(block, eng)(runner)


class Bank:

    def __init__(self, R, t, name):
        self.R, self.t, self.name = R, t, name
        self.sem = R.sem(f"bk_{name}")
        self.tok = None

    def acquire(self, eng="tensor"):
        if self.tok is not None:
            self.R.wait(eng, self.tok[0], self.tok[1])

    def release(self, eng):
        o = self.R.q[eng][-1]
        if not o.incs:
            o.inc(self.sem, 1)
        self.tok = o.tok()


class Stream:

    def __init__(self, R, name, nslots):
        self.R = R
        self.sems = [R.sem(f"{name}_{i}") for i in range(nslots)]
        self.uses = [0] * nslots

    def dma(self, eng, slot, out, in_, **kw):
        self.uses[slot] += 1
        return self.R.op(eng, "dma_start", out=out, in_=in_, **kw)\
            .inc(self.sems[slot], 16)

    def wait(self, eng, slot, use=None):
        u = self.uses[slot] if use is None else use
        self.R.wait(eng, self.sems[slot], 16 * u)

    def wait_all(self, eng):
        for i, s in enumerate(self.sems):
            self.R.wait(eng, s, 16 * self.uses[i])


def make_consts():
    c = {}
    idx = []
    for k in range(2):
        s = 128 * k + np.arange(128)[:, None]
        m = np.arange(256)[None, :]
        d = (m - s).astype(np.float32)
        d[m < s] = 1e9
        idx.append(np.ascontiguousarray(d))
    c["IDX0"], c["IDX1"] = idx
    c["QDROW"] = np.broadcast_to(
        (np.arange(256, dtype=np.float32) + 1.0)[None, :], (128, 256)).copy()
    kd = np.zeros((128, 2), dtype=np.float32)
    kd[:, 0] = 255.0 - np.arange(128)
    kd[:, 1] = 255.0 - (128 + np.arange(128))
    c["KD2"] = kd
    c["ONESR_F"] = np.ones((1, 128), dtype=np.float32)
    c["ONESC_F"] = np.ones((128, 1), dtype=np.float32)
    c["IDEN_F"] = np.eye(128, dtype=np.float32)
    cc = np.zeros((128, 8), dtype=np.float32)
    cc[:, 0] = 1e-5
    cc[:, 1] = 1e-6
    cc[:, 2] = 1e-8
    cc[:, 3] = 1.0 / 2048.0
    cc[:, 4] = 256.0
    c["CCOL"] = cc
    c["CROW"] = np.ascontiguousarray(cc[0:1, :])
    c["IVEC"] = np.array([[0.0], [-256.0], [-512.0], [-768.0]], dtype=np.float32)
    return c


def load_consts(nc, R, names, consts, sem):
    t = {}
    buf = {}
    for k in names:
        arr = consts[k]
        t[k] = nc.inline_tensor(arr, name=f"c_{k}")
        buf[k] = nc.alloc_sbuf_tensor(f"cb_{k}", list(arr.shape), F32)
        R.dma("sync", buf[k][:], t[k].ap(), sem)
    if "ONESC_F" in buf:
        buf["ONESC_BF"] = nc.alloc_sbuf_tensor("cb_ONESC_BF", [128, 1], BF16)
    if "IDEN_F" in buf:
        buf["IDEN_BF"] = nc.alloc_sbuf_tensor("cb_IDEN_BF", [128, 128], BF16)
    return buf


def finish_consts(nc, R, buf, sem):
    cbf = R.sem("cbf")
    ncast = 0
    R.wait("vector", sem)
    if "ONESC_BF" in buf:
        R.op("vector", "tensor_copy", out=buf["ONESC_BF"][:],
             in_=buf["ONESC_F"][:]).inc(cbf, 1)
        ncast += 1
    if "IDEN_BF" in buf:
        R.op("vector", "tensor_copy", out=buf["IDEN_BF"][:],
             in_=buf["IDEN_F"][:]).inc(cbf, 1)
        ncast += 1
    R.wait("tensor", cbf, ncast)


def emit_layernorm(R, S, BK, src, dst, eps_row, src_sem, tag):
    pe, dve, act = R.s["pe"], R.s["dve"], R.s["act"]
    xsq = [S["scrB"][:, 0:1024], S["scrB"][:, 1024:2048]]
    ssum = [BK["b2"].t[0:1, 0:512], BK["b3"].t[0:1, 0:512]]
    ssq = [BK["b4"].t[0:1, 0:512], BK["b5"].t[0:1, 0:512]]
    for b in ("b2", "b3", "b4", "b5"):
        BK[b].acquire()
    R.wait("scalar", src_sem)
    R.wait("tensor", src_sem)
    sq_done = R.sem(f"lnsq{tag}")
    for fc in range(FC):
        srcc = src[:, fc * NT:(fc + 1) * NT]
        p = fc % 2
        if fc >= 2:
            R.wait("scalar", pe)
        R.op("scalar", "activation", out=xsq[p][:], in_=srcc, func=AF.Square)\
            .inc(sq_done, 1)
        R.wait("tensor", sq_done)
        for half in range(2):
            rsrc = src[:, fc * NT + half * 512: fc * NT + half * 512 + 512]
            R.op("tensor", "matmul", ssum[half], S["ONESC_BF"][:], rsrc,
                 start=(fc == 0), stop=(fc == FC - 1))
            o = R.op("tensor", "matmul", ssq[half], S["ONESC_BF"][:],
                     xsq[p][:, half * 512: half * 512 + 512],
                     start=(fc == 0), stop=(fc == FC - 1))
        o.inc(pe, 1)
    mean, ex2, tmpr = S["rowA"], S["rowB"], S["rowA2"]
    rstd = ex2
    R.wait("vector", pe)
    for half in range(2):
        sl = slice(half * 512, half * 512 + 512)
        R.op("vector", "tensor_scalar", out=mean[0:1, sl], in0=ssum[half],
             scalar1=1.0 / DM, scalar2=None, op0=ALU.mult)
        R.op("vector", "tensor_scalar", out=ex2[0:1, sl], in0=ssq[half],
             scalar1=1.0 / DM, scalar2=None, op0=ALU.mult)
    o = R.op("vector", "tensor_mul", out=tmpr[0:1, 0:1024],
             in0=mean[0:1, 0:1024], in1=mean[0:1, 0:1024])
    o = R.op("vector", "tensor_sub", out=ex2[0:1, 0:1024],
             in0=ex2[0:1, 0:1024], in1=tmpr[0:1, 0:1024])
    o.inc(dve, 1)
    for b in ("b2", "b3", "b4", "b5"):
        BK[b].release("vector")
    R.wait("scalar", dve)
    R.op("scalar", "activation", out=tmpr[0:1, 0:1024], in_=ex2[0:1, 0:1024],
         func=AF.Sqrt, bias=eps_row).inc(act, 1)
    R.wait("vector", act)
    R.op("vector", "reciprocal", out=rstd[0:1, 0:1024],
         in_=tmpr[0:1, 0:1024]).inc(dve, 1)
    for b in ("b2", "b3", "b4", "b5"):
        BK[b].acquire()
    R.wait("tensor", dve)
    bmap = (("b2", 0, True), ("b4", 1, True), ("b3", 0, False), ("b5", 1, False))
    for (bn, half, is_mean) in bmap:
        sl = slice(half * 512, half * 512 + 512)
        o = R.op("tensor", "matmul", BK[bn].t[:, 0:512], S["ONESR_F"][:],
                 (mean if is_mean else rstd)[0:1, sl], start=True, stop=True)
    o.inc(pe, 1)
    meanb = S["scrB"][:, 0:1024]
    rstdb = S["scrB"][:, 1024:2048]
    R.wait("vector", pe)
    for (bn, half, is_mean) in bmap:
        dstb = (meanb if is_mean else rstdb)
        o = R.op("vector", "tensor_copy",
                 out=dstb[:, half * 512: half * 512 + 512],
                 in_=BK[bn].t[:, 0:512])
    o.inc(dve, 1)
    for b in ("b2", "b3", "b4", "b5"):
        BK[b].release("vector")
    ttmp = S["scrB"][:, 2048:3072]
    done = R.sem(f"lnd{tag}")
    for fc in range(FC):
        srcc = src[:, fc * NT:(fc + 1) * NT]
        dstc = dst[:, fc * NT:(fc + 1) * NT]
        R.op("vector", "tensor_sub", out=ttmp[:], in0=srcc, in1=meanb)
        R.op("vector", "tensor_mul", out=dstc, in0=ttmp[:], in1=rstdb)\
            .inc(done, 1)
    return done


def common_setup(nc):
    R = Rec(nc)
    R.s = {k: R.sem(k) for k in ("pe", "dve", "act", "pool", "st")}
    BK = {}
    for i in range(6):
        BK[f"b{i}"] = Bank(R, nc.alloc_psum_tensor(f"pb{i}", [128, 512], F32), f"b{i}")
    BK["b6"] = Bank(R, nc.alloc_psum_tensor("pb6", [128, 512], BF16), "b6")
    BK["b7"] = Bank(R, nc.alloc_psum_tensor("pb7", [128, 512], F32), "b7")
    return R, BK


def emit_slope_derived(R, S, BK, ld_const):
    pe, dve = R.s["pe"], R.s["dve"]
    sder = R.sem("sder")
    BK["b0"].acquire()
    R.wait("tensor", ld_const)
    R.op("tensor", "matmul", BK["b0"].t[:, 0:16], S["ONESR_F"][:],
         S["slope"][:], start=True, stop=True).inc(pe, 1)
    R.wait("vector", pe)
    R.op("vector", "tensor_scalar", out=S["sneg"][:], in0=BK["b0"].t[:, 0:16],
         scalar1=-1.0, scalar2=None, op0=ALU.mult).inc(dve, 1)
    BK["b0"].release("vector")
    R.wait("scalar", dve)
    R.op("scalar", "activation", out=S["bd_bc"][:], in_=S["sneg"][:],
         func=AF.Exp, scale=S["CCOL"][:, 4:5]).inc(sder, 1)
    return sder



def build_prog_A(consts):
    nc = bass.Bass("TRN2", target_bir_lowering=False, debug=False)
    xT = nc.dram_tensor("xT", [DM, NT], F32, kind="ExternalInput").ap()
    slope = nc.dram_tensor("slope", [1, NH], F32, kind="ExternalInput").ap()
    wqkvu = nc.dram_tensor("wqkvu", [128, 64 * 2048], F32, kind="ExternalInput").ap()
    w1s = nc.dram_tensor("w1s", [128, 11264], F32, kind="ExternalInput").ap()
    w2s = nc.dram_tensor("w2s", [128, 11264], F32, kind="ExternalInput").ap()
    w3s = nc.dram_tensor("w3s", [128, 11264], F32, kind="ExternalInput").ap()
    kv_out = nc.dram_tensor("kv_out", [128, NH * 128], F32, kind="ExternalOutput").ap()
    gpart = nc.dram_tensor("gpart", [1, 4], F32, kind="ExternalOutput").ap()
    qs_out = nc.dram_tensor("qs_out", [128, NH * NT], BF16, kind="ExternalOutput").ap()
    o_out = nc.dram_tensor("o_out", [128, NH * NT], BF16, kind="ExternalOutput").ap()
    u_out = nc.dram_tensor("u_out", [128, FC * NT], BF16, kind="ExternalOutput").ap()

    R, BK = common_setup(nc)
    s = R.s
    A = nc.alloc_sbuf_tensor
    AR_LN = A("ar_ln", [128, FC * NT], BF16)
    AR_Q = A("ar_q", [128, NH * NT], BF16)
    AR_K = A("ar_k", [128, NH * NT], BF16)
    AR_V = A("ar_v", [128, TC * DM], BF16)
    SCRB = A("scrb", [128, 8192], BF16)
    OSTGA = A("ostga", [128, 2048], BF16)
    SCRF = A("scrf", [128, 2880], F32)
    WG = [A(f"wg{i}", [128, 1024], F32) for i in range(2)]
    WGB = [A(f"wgb{i}", [128, 1024], BF16) for i in range(4)]
    KVOUT = A("kvout", [128, NH * 128], F32)
    S = {
        "slope": A("s_slope", [1, NH], F32),
        "sneg": A("s_sneg", [128, NH], F32),
        "bd_bc": A("s_bdbc", [128, NH], F32),
        "rowA": A("s_rowA", [1, 1024], F32),
        "rowA2": A("s_rowA2", [1, 1024], F32),
        "rowB": A("s_rowB", [1, 1024], F32),
        "gacc": A("s_gacc", [128, 4], F32),
        "gtmp": A("s_gtmp", [128, 4], F32),
        "gstg": A("s_gstg", [1, 4], F32),
        "kvbf": A("s_kvbf", [128, 128], BF16),
        "scrB": SCRB,
    }
    ld_const = R.sem("ldconst")
    CB = load_consts(nc, R,
                     ("IDX0", "IDX1", "QDROW", "KD2", "ONESR_F",
                      "ONESC_F", "IDEN_F", "CCOL", "CROW"), consts, ld_const)
    S.update(CB)
    R.dma("sync", S["slope"][:], slope, ld_const)
    finish_consts(nc, R, S, ld_const)
    sder = emit_slope_derived(R, S, BK, ld_const)

    xc = [SCRF[:, 0:1024], SCRF[:, 1024:2048]]
    ldx = Stream(R, "ldx", 2)
    cast_done = R.sem("castdone")
    for fc in range(FC):
        p = fc % 2
        if fc >= 2:
            R.wait("sync", cast_done, fc - 1)
        ldx.dma("sync", p, xc[p][:], xT[fc * 128:(fc + 1) * 128, :])
        ldx.wait("vector", p)
        R.op("vector", "tensor_copy", out=AR_LN[:, fc * NT:(fc + 1) * NT],
             in_=xc[p][:]).inc(cast_done, 1)
    ln1_done = emit_layernorm(R, S, BK, AR_LN, AR_LN, S["CROW"][0:1, 0:1],
                              cast_done, "1")

    wld = Stream(R, "wld", 2)
    wcast = R.sem("wcast")
    sevq = R.sem("sevq")
    sevk = R.sem("sevk")
    vev = R.sem("vev")
    trev = R.sem("trev")
    uev = Stream(R, "uev", 2)
    sevu = R.sem("sevu")
    vstg = [SCRB[:, 3072 + i * 512: 3584 + i * 512] for i in range(2)]
    ustg = [SCRB[:, 4096 + i * 512: 4608 + i * 512] for i in range(2)]
    granule_pe_after = {}
    vtr_pe_after = {}
    for j in range(64):
        for half in range(2):
            g = 2 * j + half
            bgi = g % 4
            if g >= 2:
                R.wait("sync", wcast, g - 1)
            wld.dma("sync", g % 2, WG[g % 2][:],
                    wqkvu[:, j * 2048 + half * 1024: j * 2048 + (half + 1) * 1024])
            wld.wait("vector", g % 2)
            if j >= 2:
                R.wait("vector", s["pe"], granule_pe_after[j - 2])
            R.op("vector", "tensor_copy", out=WGB[bgi][:], in_=WG[g % 2][:])\
                .inc(wcast, 1)
        R.wait("tensor", wcast, 2 * (j + 1))
        if j == 0:
            R.wait("tensor", ln1_done, FC)
        b_lo, b_hi = WGB[(2 * j) % 4], WGB[(2 * j + 1) % 4]
        pnames = ("b0", "b1") if j % 2 == 0 else ("b2", "b3")
        for tc_ in range(2):
            bn = pnames[tc_]
            BK[bn].acquire()
            for kc in range(16):
                wb = b_lo if kc < 8 else b_hi
                off = (kc % 8) * 128
                o = R.op("tensor", "matmul", BK[bn].t[:, 0:512],
                         wb[:, off:off + 128],
                         AR_LN[:, kc * NT + tc_ * 512: kc * NT + tc_ * 512 + 512],
                         start=(kc == 0), stop=(kc == 15))
            o.inc(s["pe"], 1)
            pe_now = s["pe"].v
            if j < 16:
                dst = AR_Q[:, j * NT + tc_ * 512: j * NT + tc_ * 512 + 512]
                R.wait("scalar", s["pe"], pe_now)
                R.op("scalar", "activation", out=dst, in_=BK[bn].t[:, 0:512],
                     func=AF.Silu).inc(sevq, 1)
                BK[bn].release("scalar")
            elif j < 32:
                jj = j - 16
                dst = AR_K[:, jj * NT + tc_ * 512: jj * NT + tc_ * 512 + 512]
                R.wait("scalar", s["pe"], pe_now)
                R.op("scalar", "activation", out=dst, in_=BK[bn].t[:, 0:512],
                     func=AF.Silu).inc(sevk, 1)
                BK[bn].release("scalar")
            elif j < 48:
                jj = j - 32
                R.wait("vector", s["pe"], pe_now)
                key = tc_
                if key in vtr_pe_after:
                    R.wait("vector", s["pe"], vtr_pe_after[key])
                R.op("vector", "tensor_copy", out=vstg[tc_][:],
                     in_=BK[bn].t[:, 0:512]).inc(vev, 1)
                BK[bn].release("vector")
                BK["b6"].acquire()
                R.wait("tensor", vev)
                for i4 in range(4):
                    o = R.op("tensor", "transpose",
                             BK["b6"].t[:, i4 * 128:(i4 + 1) * 128],
                             vstg[tc_][:, i4 * 128:(i4 + 1) * 128],
                             S["IDEN_BF"][:])
                o.inc(s["pe"], 1)
                vtr_pe_after[key] = s["pe"].v
                R.wait("vector", s["pe"], s["pe"].v)
                for i4 in range(4):
                    tch = tc_ * 4 + i4
                    R.op("vector", "tensor_copy",
                         out=AR_V[:, tch * DM + jj * 128: tch * DM + (jj + 1) * 128],
                         in_=BK["b6"].t[:, i4 * 128:(i4 + 1) * 128]).inc(trev, 1)
                BK["b6"].release("vector")
            else:
                jj = j - 48
                un = jj * 2 + tc_
                R.wait("vector", s["pe"], pe_now)
                if un >= 2:
                    uev.wait("vector", un % 2, un // 2)
                R.op("vector", "tensor_copy", out=ustg[un % 2][:],
                     in_=BK[bn].t[:, 0:512]).inc(sevu, 1)
                BK[bn].release("vector")
                R.wait("scalar", sevu)
                uev.dma("scalar", un % 2,
                        u_out[:, jj * NT + tc_ * 512: jj * NT + tc_ * 512 + 512],
                        ustg[un % 2][:])
        granule_pe_after[j] = s["pe"].v

    gb = [SCRF[:, 0:1408], SCRF[:, 1408:2816]]
    gld = Stream(R, "gld", 2)
    gred = R.sem("gred")
    R.wait("gpsimd", cast_done, FC)
    for wi, wsh in enumerate((w1s, w2s, w3s)):
        for kk in range(8):
            nch = wi * 8 + kk
            if nch >= 2:
                R.wait("gpsimd", gred, nch - 1)
            gld.dma("gpsimd", nch % 2, gb[nch % 2][:],
                    wsh[:, kk * 1408:(kk + 1) * 1408])

    gamma_state = {"done": 0, "init": False}
    gbar = R.sem("gbar")

    def emit_gamma_chunks(upto):
        if not gamma_state["init"]:
            R.op("vector", "memset", S["gacc"][:], 0.0)
            gamma_state["init"] = True
        while gamma_state["done"] < min(upto, 24):
            nch = gamma_state["done"]
            wi = nch // 8
            gld.wait("vector", nch % 2, nch // 2 + 1)
            R.op("vector", "tensor_reduce", out=S["gtmp"][:, wi:wi + 1],
                 in_=gb[nch % 2][:], axis=AX, op=ALU.add,
                 apply_absolute_value=True).inc(gbar, 1)
            R.wait("vector", gbar, nch + 1)
            R.op("vector", "tensor_tensor", out=S["gacc"][:, wi:wi + 1],
                 in0=S["gacc"][:, wi:wi + 1], in1=S["gtmp"][:, wi:wi + 1],
                 op=ALU.add).inc(gred, 1)
            gamma_state["done"] += 1

    maskb = [SCRB[:, 5120 + i * 512: 5632 + i * 512] for i in range(2)]
    qdb = [SCRB[:, 6144 + i * 256: 6400 + i * 256] for i in range(2)]
    ktokb = [SCRB[:, 6656 + i * 256: 6912 + i * 256] for i in range(2)]
    qkmb = [SCRB[:, 7168 + i * 512: 7680 + i * 512] for i in range(2)]
    kdb = [SCRF[:, 2816 + 2 * i: 2818 + 2 * i] for i in range(2)]
    smask = R.sem("smask")
    sqs = R.sem("sqs")
    sktok = R.sem("sktok")
    sqkm = R.sem("sqkm")
    skv = R.sem("skv")
    skvbf = R.sem("skvbf")
    soev = R.sem("soev")
    soq = Stream(R, "soq", 2)
    q_total, k_total, v_total = sevq.v, sevk.v, trev.v
    R.wait("vector", s["pe"], granule_pe_after[63])
    R.wait("vector", sder, 1)
    R.wait("tensor", sevq, q_total)
    R.wait("tensor", sevk, k_total)
    R.wait("tensor", trev, v_total)
    R.wait("vector", sevq, q_total)
    C_pe_after = {}
    diag_pe_after = {}
    for h in range(NH):
        hp = h % 2
        R.wait("scalar", sder, 1)
        if h >= 2:
            R.wait("scalar", sqkm, 4 * (h - 1))
            R.wait("scalar", sqs, h - 1)
            R.wait("scalar", sktok, 4 * (h - 1))
        R.op("scalar", "activation", out=maskb[hp][:, 0:256], in_=S["IDX0"][:],
             func=AF.Exp, scale=S["sneg"][:, h:h + 1])
        R.op("scalar", "activation", out=maskb[hp][:, 256:512], in_=S["IDX1"][:],
             func=AF.Exp, scale=S["sneg"][:, h:h + 1])
        R.op("scalar", "activation", out=qdb[hp][:], in_=S["QDROW"][:],
             func=AF.Exp, scale=S["sneg"][:, h:h + 1])
        R.op("scalar", "activation", out=kdb[hp][:], in_=S["KD2"][:],
             func=AF.Exp, scale=S["sneg"][:, h:h + 1]).inc(smask, 1)
        R.wait("vector", smask, h + 1)
        qd = qdb[hp]
        qrep = bass.AP(qd.tensor, qd.offset, [[qd.ap[0][0], 128], [0, 4], [1, 256]])
        R.op("vector", "tensor_tensor",
             out=AR_LN[:, h * NT:(h + 1) * NT].rearrange("p (b m) -> p b m", b=4),
             in0=AR_Q[:, h * NT:(h + 1) * NT].rearrange("p (b m) -> p b m", b=4),
             in1=qrep, op=ALU.mult).inc(sqs, 1)
        for i in range(NBLK):
            blk = h * NBLK + i
            BK["b6"].acquire()
            o = R.op("tensor", "transpose", BK["b6"].t[:, 0:128],
                     AR_K[:, h * NT + i * 256: h * NT + i * 256 + 128],
                     S["IDEN_BF"][:])
            o = R.op("tensor", "transpose", BK["b6"].t[:, 128:256],
                     AR_K[:, h * NT + i * 256 + 128: h * NT + i * 256 + 256],
                     S["IDEN_BF"][:])
            o.inc(s["pe"], 1)
            tr_pe = s["pe"].v
            R.wait("vector", s["pe"], tr_pe)
            R.wait("vector", smask, h + 1)
            prevu = blk - 1 if i > 0 else ((h - 2) * NBLK + 3 if h >= 2 else None)
            if prevu is not None and prevu in C_pe_after:
                R.wait("vector", s["pe"], C_pe_after[prevu])
            R.op("vector", "tensor_scalar", out=ktokb[hp][:, 0:128],
                 in0=BK["b6"].t[:, 0:128], scalar1=kdb[hp][:, 0:1],
                 scalar2=None, op0=ALU.mult)
            R.op("vector", "tensor_scalar", out=ktokb[hp][:, 128:256],
                 in0=BK["b6"].t[:, 128:256], scalar1=kdb[hp][:, 1:2],
                 scalar2=None, op0=ALU.mult).inc(sktok, 1)
            BK["b6"].release("vector")
            BK["b7"].acquire()
            R.wait("tensor", sktok, blk + 1)
            tch = 2 * i
            R.op("tensor", "matmul", BK["b7"].t[:, 0:128],
                 ktokb[hp][:, 0:128],
                 AR_V[:, tch * DM + h * 128: tch * DM + (h + 1) * 128],
                 start=True, stop=False)
            R.op("tensor", "matmul", BK["b7"].t[:, 0:128],
                 ktokb[hp][:, 128:256],
                 AR_V[:, (tch + 1) * DM + h * 128: (tch + 1) * DM + (h + 1) * 128],
                 start=False, stop=True).inc(s["pe"], 1)
            C_pe_after[blk] = s["pe"].v
            BK["b0"].acquire()
            R.op("tensor", "matmul", BK["b0"].t[:, 0:256],
                 AR_K[:, h * NT + i * 256: h * NT + i * 256 + 128],
                 AR_Q[:, h * NT + i * 256: h * NT + (i + 1) * 256],
                 start=True, stop=True)
            R.op("tensor", "matmul", BK["b0"].t[:, 256:512],
                 AR_K[:, h * NT + i * 256 + 128: h * NT + i * 256 + 256],
                 AR_Q[:, h * NT + i * 256: h * NT + (i + 1) * 256],
                 start=True, stop=True).inc(s["pe"], 1)
            qk_pe = s["pe"].v
            R.wait("vector", s["pe"], qk_pe)
            prevq = blk - 1 if i > 0 else ((h - 2) * NBLK + 3 if h >= 2 else None)
            if prevq is not None and prevq in diag_pe_after:
                R.wait("vector", s["pe"], diag_pe_after[prevq])
            R.op("vector", "tensor_tensor", out=qkmb[hp][:, 0:256],
                 in0=BK["b0"].t[:, 0:256], in1=maskb[hp][:, 0:256], op=ALU.mult)
            R.op("vector", "tensor_tensor", out=qkmb[hp][:, 256:512],
                 in0=BK["b0"].t[:, 256:512], in1=maskb[hp][:, 256:512],
                 op=ALU.mult).inc(sqkm, 1)
            BK["b0"].release("vector")
            ob = "b1" if i % 2 == 0 else "b2"
            BK[ob].acquire()
            if i > 0:
                R.wait("tensor", skvbf, h * (NBLK - 1) + i)
                R.wait("tensor", sqs, h + 1)
                R.op("tensor", "matmul", BK[ob].t[:, 0:256], S["kvbf"][:],
                     AR_LN[:, h * NT + i * 256: h * NT + (i + 1) * 256],
                     start=True, stop=False)
            R.wait("tensor", sqkm, blk + 1)
            R.op("tensor", "matmul", BK[ob].t[:, 0:256],
                 AR_V[:, tch * DM + h * 128: tch * DM + (h + 1) * 128],
                 qkmb[hp][:, 0:256], start=(i == 0), stop=False)
            R.op("tensor", "matmul", BK[ob].t[:, 0:256],
                 AR_V[:, (tch + 1) * DM + h * 128: (tch + 1) * DM + (h + 1) * 128],
                 qkmb[hp][:, 256:512], start=False, stop=True).inc(s["pe"], 1)
            diag_pe_after[blk] = s["pe"].v
            R.wait("vector", s["pe"], s["pe"].v)
            kvslice = KVOUT[:, h * 128:(h + 1) * 128]
            if i == 0:
                R.op("vector", "tensor_copy", out=kvslice,
                     in_=BK["b7"].t[:, 0:128]).inc(skv, 1)
            else:
                R.op("vector", "scalar_tensor_tensor", out=kvslice, in0=kvslice,
                     scalar=S["bd_bc"][:, h:h + 1], in1=BK["b7"].t[:, 0:128],
                     op0=ALU.mult, op1=ALU.add).inc(skv, 1)
            BK["b7"].release("vector")
            if i < NBLK - 1:
                R.op("vector", "tensor_copy", out=S["kvbf"][:], in_=kvslice)\
                    .inc(skvbf, 1)
            if i == 0 and h >= 2:
                soq.wait("vector", hp, h // 2)
            R.op("vector", "tensor_copy",
                 out=OSTGA[:, hp * 1024 + i * 256: hp * 1024 + (i + 1) * 256],
                 in_=BK[ob].t[:, 0:256]).inc(soev, 1)
            BK[ob].release("vector")
            if i == NBLK - 1:
                R.wait("scalar", soev)
                soq.dma("scalar", hp, o_out[:, h * NT:(h + 1) * NT],
                        OSTGA[:, hp * 1024:(hp + 1) * 1024])
                emit_gamma_chunks((h + 1) * 24 // NH)

    emit_gamma_chunks(24)
    BK["b7"].acquire()
    R.wait("tensor", gred)
    R.op("tensor", "matmul", BK["b7"].t[0:1, 0:4], S["ONESC_F"][:],
         S["gacc"][:], start=True, stop=True).inc(s["pe"], 1)
    R.wait("vector", s["pe"])
    R.op("vector", "tensor_copy", out=S["gstg"][:], in_=BK["b7"].t[0:1, 0:4])\
        .inc(s["dve"], 1)
    BK["b7"].release("vector")
    gstg_done = s["dve"].v

    st = s["st"]
    R.wait("scalar", skv)
    R.wait("scalar", sqs)
    R.wait("scalar", s["dve"], gstg_done)
    R.dma("scalar", kv_out, KVOUT[:], st)
    R.dma("scalar", gpart, S["gstg"][:], st)
    R.dma("scalar", qs_out, AR_LN[:], st)
    R.wait("scalar", st)
    soq.wait_all("scalar")
    uev.wait_all("scalar")
    R.replay()
    return nc



def build_prog_B(consts):
    nc = bass.Bass("TRN2", target_bir_lowering=False, debug=False)
    xT = nc.dram_tensor("xT", [DM, NT], F32, kind="ExternalInput").ap()
    slope = nc.dram_tensor("slope", [1, NH], F32, kind="ExternalInput").ap()
    cvec = nc.dram_tensor("cvec", [8, 1], F32, kind="ExternalInput").ap()
    kv_all = nc.dram_tensor("kv_all", [8 * 128, NH * 128], F32, kind="ExternalInput").ap()
    gparts = nc.dram_tensor("gparts", [8, 4], F32, kind="ExternalInput").ap()
    qs_in = nc.dram_tensor("qs_in", [128, NH * NT], BF16, kind="ExternalInput").ap()
    o_in = nc.dram_tensor("o_in", [128, NH * NT], BF16, kind="ExternalInput").ap()
    u_in = nc.dram_tensor("u_in", [128, FC * NT], BF16, kind="ExternalInput").ap()
    wout = nc.dram_tensor("wout", [128, 16 * 2048], F32, kind="ExternalInput").ap()
    w1t = nc.dram_tensor("w1t", [128, 44 * 2048], F32, kind="ExternalInput").ap()
    w2t = nc.dram_tensor("w2t", [128, 44 * 2048], F32, kind="ExternalInput").ap()
    w3t = nc.dram_tensor("w3t", [128, 16 * 5632], F32, kind="ExternalInput").ap()
    out_c = nc.dram_tensor("out_c", [DM, NT], F32, kind="ExternalOutput").ap()
    x1_spill = nc.dram_tensor("x1_spill", [128, FC * NT], BF16, kind="Internal").ap()
    wgt_bounce = nc.dram_tensor("wgt_bounce", [1, 128], F32, kind="Internal").ap()
    bdi_bounce = nc.dram_tensor("bdi_bounce", [1, 64], F32, kind="Internal").ap()

    R, BK = common_setup(nc)
    s = R.s
    A = nc.alloc_sbuf_tensor
    AR_A = A("ar_a", [128, NH * NT], BF16)
    AR_B = A("ar_b", [128, NH * NT], BF16)
    AR_C = A("ar_c", [128, NH * NT], BF16)
    AR_D = A("ar_d", [128, NH * NT], BF16)
    SCRB = A("scrb", [128, 6912], BF16)
    SCRF = A("scrf", [128, 4096], F32)
    WQ = A("wq", [128, 5632], BF16)
    KVIN = A("kvin", [128, NH * 128], F32)
    KVINB = A("kvinb", [128, NH * 128], BF16)
    OSTG = [A(f"ostg{i}", [128, 512], F32) for i in range(2)]
    S = {
        "slope": A("s_slope", [1, NH], F32),
        "sneg": A("s_sneg", [128, NH], F32),
        "bd_bc": A("s_bdbc", [128, NH], F32),
        "cvec": A("s_cvec", [8, 1], F32),
        "wgt": A("s_wgt", [8, 16], F32),
        "wgtrow": A("s_wgtrow", [1, 128], F32),
        "wgtbc": A("s_wgtbc", [128, 128], F32),
        "bdi4": A("s_bdi4", [4, 16], F32),
        "bdirow": A("s_bdirow", [1, 64], F32),
        "bdibc": A("s_bdibc", [128, 64], F32),
        "slope8": A("s_slope8", [8, 16], F32),
        "slope4": A("s_slope4", [4, 16], F32),
        "rowA": A("s_rowA", [1, 1024], F32),
        "rowA2": A("s_rowA2", [1, 1024], F32),
        "rowB": A("s_rowB", [1, 1024], F32),
        "grow": A("s_grow", [8, 4], F32),
        "gsum": A("s_gsum", [1, 4], F32),
        "th": A("s_th", [1, 4], F32),
        "nth": A("s_nth", [1, 4], F32),
        "thb": A("s_thb", [128, 4], F32),
        "nthb": A("s_nthb", [128, 4], F32),
        "scrB": SCRB,
    }
    ld_const = R.sem("ldconst")
    CB = load_consts(nc, R,
                     ("QDROW", "ONESR_F", "ONESC_F", "CCOL",
                      "CROW", "IVEC"), consts, ld_const)
    S.update(CB)
    R.dma("sync", S["slope"][:], slope, ld_const)
    R.dma("sync", S["cvec"][:], cvec, ld_const)
    R.dma("sync", S["grow"][:], gparts, ld_const)
    finish_consts(nc, R, S, ld_const)
    sder = emit_slope_derived(R, S, BK, ld_const)

    ldbig = R.sem("ldbig")
    R.dma("sync", AR_A[:], qs_in, ldbig)
    R.dma("sync", AR_B[:], o_in, ldbig)

    pe, dve, act = s["pe"], s["dve"], s["act"]
    BK["b4"].acquire()
    R.wait("tensor", ld_const)
    R.op("tensor", "matmul", BK["b4"].t[0:8, 0:16], S["ONESR_F"][0:1, 0:8],
         S["slope"][:], start=True, stop=True)
    R.op("tensor", "matmul", BK["b4"].t[0:4, 16:32], S["ONESR_F"][0:1, 0:4],
         S["slope"][:], start=True, stop=True).inc(pe, 1)
    R.wait("vector", pe)
    R.op("vector", "tensor_copy", out=S["slope8"][:], in_=BK["b4"].t[0:8, 0:16])
    R.op("vector", "tensor_copy", out=S["slope4"][:], in_=BK["b4"].t[0:4, 16:32])\
        .inc(dve, 1)
    BK["b4"].release("vector")
    R.wait("scalar", dve)
    R.op("scalar", "activation", out=S["wgt"][:], in_=S["slope8"][:],
         func=AF.Exp, scale=S["cvec"][:])
    R.op("scalar", "activation", out=S["bdi4"][:], in_=S["slope4"][:],
         func=AF.Exp, scale=S["IVEC"][:]).inc(act, 1)
    bounce = R.sem("bounce")
    R.wait("scalar", act)
    R.dma("scalar", wgt_bounce.rearrange("a (b c) -> (a b) c", b=8),
          S["wgt"][:], bounce)
    R.dma("scalar", bdi_bounce.rearrange("a (b c) -> (a b) c", b=4),
          S["bdi4"][:], bounce)
    R.wait("sync", bounce, 32)
    R.dma("sync", S["wgtrow"][:], wgt_bounce, bounce)
    R.dma("sync", S["bdirow"][:], bdi_bounce, bounce)
    BK["b4"].acquire()
    R.wait("tensor", bounce, 64)
    R.op("tensor", "matmul", BK["b4"].t[:, 0:128], S["ONESR_F"][:],
         S["wgtrow"][:], start=True, stop=True)
    R.op("tensor", "matmul", BK["b4"].t[:, 128:192], S["ONESR_F"][:],
         S["bdirow"][:], start=True, stop=True).inc(pe, 1)
    R.wait("vector", pe)
    R.op("vector", "tensor_copy", out=S["wgtbc"][:], in_=BK["b4"].t[:, 0:128])
    R.op("vector", "tensor_copy", out=S["bdibc"][:], in_=BK["b4"].t[:, 128:192])\
        .inc(dve, 1)
    BK["b4"].release("vector")
    wgt_ready = dve.v

    BK["b5"].acquire()
    R.wait("tensor", ld_const)
    R.op("tensor", "matmul", BK["b5"].t[0:1, 0:4], S["ONESC_F"][0:8, 0:1],
         S["grow"][:], start=True, stop=True).inc(pe, 1)
    R.wait("vector", pe)
    R.op("vector", "tensor_copy", out=S["gsum"][:], in_=BK["b5"].t[0:1, 0:4])\
        .inc(dve, 1)
    BK["b5"].release("vector")
    R.op("vector", "tensor_scalar", out=S["th"][:], in0=S["gsum"][:],
         scalar1=1.0 / NW_ELEMS, scalar2=1e-8, op0=ALU.mult, op1=ALU.add)
    R.op("vector", "tensor_scalar", out=S["th"][:], in0=S["th"][:],
         scalar1=0.5, scalar2=None, op0=ALU.mult)
    R.op("vector", "tensor_scalar", out=S["nth"][:], in0=S["th"][:],
         scalar1=-1.0, scalar2=None, op0=ALU.mult).inc(dve, 1)
    BK["b5"].acquire()
    R.wait("tensor", dve)
    R.op("tensor", "matmul", BK["b5"].t[:, 0:4], S["ONESR_F"][:], S["th"][:],
         start=True, stop=True)
    R.op("tensor", "matmul", BK["b5"].t[:, 4:8], S["ONESR_F"][:], S["nth"][:],
         start=True, stop=True).inc(pe, 1)
    R.wait("vector", pe)
    R.op("vector", "tensor_copy", out=S["thb"][:], in_=BK["b5"].t[:, 0:4])
    R.op("vector", "tensor_copy", out=S["nthb"][:], in_=BK["b5"].t[:, 4:8])\
        .inc(dve, 1)
    BK["b5"].release("vector")
    th_ready = dve.v

    kvg = [SCRF[:, 0:2048], SCRF[:, 2048:4096]]
    kld = Stream(R, "kld", 2)
    kused = R.sem("kused")
    R.wait("vector", dve, wgt_ready)
    for cp in range(8):
        p = cp % 2
        if cp >= 2:
            R.wait("sync", kused, cp - 1)
        kld.dma("sync", p, kvg[p][:], kv_all[cp * 128:(cp + 1) * 128, :])
        kld.wait("vector", p)
        for h in range(NH):
            dst = KVIN[:, h * 128:(h + 1) * 128]
            srcs = kvg[p][:, h * 128:(h + 1) * 128]
            wcol = S["wgtbc"][:, cp * 16 + h: cp * 16 + h + 1]
            if cp == 0:
                R.op("vector", "tensor_scalar", out=dst, in0=srcs,
                     scalar1=wcol, scalar2=None, op0=ALU.mult)
            else:
                R.op("vector", "scalar_tensor_tensor", out=dst, in0=srcs,
                     scalar=wcol, in1=dst, op0=ALU.mult, op1=ALU.add)
        R.q["vector"][-1].inc(kused, 1)
    R.op("vector", "tensor_copy", out=KVINB[:], in_=KVIN[:]).inc(dve, 1)
    kvin_ready = dve.v

    scorr = R.sem("scorr")
    R.wait("tensor", ldbig, 32)
    R.wait("tensor", dve, kvin_ready)
    for h in range(NH):
        for i in range(NBLK):
            ob = "b0" if (h * NBLK + i) % 2 == 0 else "b1"
            BK[ob].acquire()
            R.op("tensor", "matmul", BK[ob].t[:, 0:256],
                 KVINB[:, h * 128:(h + 1) * 128],
                 AR_A[:, h * NT + i * 256: h * NT + (i + 1) * 256],
                 start=True, stop=True).inc(pe, 1)
            R.wait("vector", pe)
            R.op("vector", "scalar_tensor_tensor",
                 out=AR_B[:, h * NT + i * 256: h * NT + (i + 1) * 256],
                 in0=BK[ob].t[:, 0:256],
                 scalar=S["bdibc"][:, i * 16 + h: i * 16 + h + 1],
                 in1=AR_B[:, h * NT + i * 256: h * NT + (i + 1) * 256],
                 op0=ALU.mult, op1=ALU.add).inc(scorr, 1)
            BK[ob].release("vector")

    osq = [SCRB[:, 0:1024], SCRB[:, 1024:2048]]
    sq2 = R.sem("sq2")
    R.wait("scalar", scorr)
    BK["b4"].acquire()
    BK["b5"].acquire()
    for fc in range(FC):
        p = fc % 2
        if fc >= 2:
            R.wait("scalar", pe)
        R.op("scalar", "activation", out=osq[p][:],
             in_=AR_B[:, fc * NT:(fc + 1) * NT], func=AF.Square).inc(sq2, 1)
        R.wait("tensor", sq2)
        for half in range(2):
            o = R.op("tensor", "matmul",
                     (BK["b4"] if half == 0 else BK["b5"]).t[0:1, 0:512],
                     S["ONESC_BF"][:], osq[p][:, half * 512: half * 512 + 512],
                     start=(fc == 0), stop=(fc == FC - 1))
        o.inc(pe, 1)
    R.wait("scalar", pe)
    R.op("scalar", "activation", out=S["rowA2"][0:1, 0:512],
         in_=BK["b4"].t[0:1, 0:512], func=AF.Sqrt,
         scale=S["CROW"][0:1, 3:4], bias=S["CROW"][0:1, 1:2])
    R.op("scalar", "activation", out=S["rowA2"][0:1, 512:1024],
         in_=BK["b5"].t[0:1, 0:512], func=AF.Sqrt,
         scale=S["CROW"][0:1, 3:4], bias=S["CROW"][0:1, 1:2]).inc(act, 1)
    BK["b4"].release("scalar")
    BK["b5"].release("scalar")
    R.wait("vector", act)
    R.op("vector", "reciprocal", out=S["rowA"][0:1, 0:1024],
         in_=S["rowA2"][0:1, 0:1024]).inc(dve, 1)
    BK["b4"].acquire()
    BK["b5"].acquire()
    R.wait("tensor", dve)
    R.op("tensor", "matmul", BK["b4"].t[:, 0:512], S["ONESR_F"][:],
         S["rowA"][0:1, 0:512], start=True, stop=True)
    R.op("tensor", "matmul", BK["b5"].t[:, 0:512], S["ONESR_F"][:],
         S["rowA"][0:1, 512:1024], start=True, stop=True).inc(pe, 1)
    rstdb = SCRB[:, 2048:3072]
    R.wait("vector", pe)
    R.op("vector", "tensor_copy", out=rstdb[:, 0:512], in_=BK["b4"].t[:, 0:512])
    R.op("vector", "tensor_copy", out=rstdb[:, 512:1024],
         in_=BK["b5"].t[:, 0:512]).inc(dve, 1)
    BK["b4"].release("vector")
    BK["b5"].release("vector")
    uload = R.sem("uload")
    R.dma("sync", AR_C[:], u_in, uload)
    gdone = R.sem("gdone")
    ttmp = SCRB[:, 3072:4096]
    R.wait("vector", uload, 16)
    for fc in range(FC):
        R.op("vector", "tensor_mul", out=ttmp[:],
             in0=AR_B[:, fc * NT:(fc + 1) * NT], in1=rstdb[:])
        R.op("vector", "tensor_mul", out=AR_D[:, fc * NT:(fc + 1) * NT],
             in0=ttmp[:], in1=AR_C[:, fc * NT:(fc + 1) * NT]).inc(gdone, 1)

    WGf = [SCRF[:, 0:1024], SCRF[:, 1024:2048]]
    WGb = [WQ[:, i * 1024:(i + 1) * 1024] for i in range(4)]
    xch = [SCRF[:, 2048:3072], SCRF[:, 3072:4096]]
    wldB = Stream(R, "wldB", 2)
    wcastB = R.sem("wcastB")
    xld = Stream(R, "xld", 2)
    x1done = R.sem("x1done")
    R.wait("sync", kused, 8)
    granule_pe_after = {}
    for j in range(16):
        p = j % 2
        if j >= 2:
            R.wait("sync", x1done, 2 * (j - 1))
        xld.dma("sync", p, xch[p][:], xT[j * 128:(j + 1) * 128, :])
        for half in range(2):
            g = 2 * j + half
            if g >= 2:
                R.wait("sync", wcastB, g - 1)
            wldB.dma("sync", g % 2, WGf[g % 2][:],
                     wout[:, j * 2048 + half * 1024: j * 2048 + (half + 1) * 1024])
            wldB.wait("vector", g % 2)
            if j >= 2:
                R.wait("vector", pe, granule_pe_after[j - 2])
            R.op("vector", "tensor_copy", out=WGb[g % 4][:], in_=WGf[g % 2][:])\
                .inc(wcastB, 1)
        R.wait("tensor", wcastB, 2 * (j + 1))
        if j == 0:
            R.wait("tensor", gdone, FC)
        b_lo, b_hi = WGb[(2 * j) % 4], WGb[(2 * j + 1) % 4]
        pnames = ("b0", "b1") if j % 2 == 0 else ("b2", "b3")
        for tc_ in range(2):
            bn = pnames[tc_]
            BK[bn].acquire()
            for kc in range(16):
                wb = b_lo if kc < 8 else b_hi
                off = (kc % 8) * 128
                o = R.op("tensor", "matmul", BK[bn].t[:, 0:512],
                         wb[:, off:off + 128],
                         AR_D[:, kc * NT + tc_ * 512: kc * NT + tc_ * 512 + 512],
                         start=(kc == 0), stop=(kc == 15))
            o.inc(pe, 1)
            R.wait("vector", pe)
            xld.wait("vector", p)
            R.op("vector", "tensor_tensor",
                 out=AR_B[:, j * NT + tc_ * 512: j * NT + tc_ * 512 + 512],
                 in0=BK[bn].t[:, 0:512],
                 in1=xch[p][:, tc_ * 512: tc_ * 512 + 512],
                 op=ALU.add).inc(x1done, 1)
            BK[bn].release("vector")
        granule_pe_after[j] = pe.v

    ln2_done = emit_layernorm(R, S, BK, AR_B, AR_A, S["CROW"][0:1, 0:1],
                              x1done, "2")
    spill = R.sem("spill")
    R.wait("scalar", ln2_done)
    R.dma("scalar", x1_spill, AR_B[:], spill)

    def hidden_ap(kc, lo, width):
        if kc < 16:
            return AR_B[:, kc * NT + lo: kc * NT + lo + width]
        if kc < 32:
            return AR_C[:, (kc - 16) * NT + lo: (kc - 16) * NT + lo + width]
        return AR_D[:, (kc - 32) * NT + lo: (kc - 32) * NT + lo + width]

    f32bufs = {"w1": [SCRF[:, 0:1024], SCRF[:, 1024:2048]],
               "w2": [SCRF[:, 2048:3072], SCRF[:, 3072:4096]]}
    wqbufs = {"w1": [WQ[:, 0:1024], WQ[:, 1024:2048]],
              "w2": [WQ[:, 2048:3072], WQ[:, 3072:4096]]}
    tsD = SCRB[:, 0:2048]
    tsP = SCRB[:, 2048:4096]
    sgb = [SCRB[:, 4096 + i * 512: 4608 + i * 512] for i in range(2)]
    fld = {"w1": Stream(R, "fld1", 2), "w2": Stream(R, "fld2", 2)}
    qbar = {"vector": R.sem("qbarD"), "gpsimd": R.sem("qbarP")}
    qbar_n = {"vector": 0, "gpsimd": 0}
    fq = {"w1": R.sem("fq1"), "w2": R.sem("fq2")}
    sgd = R.sem("sgd")
    hid = R.sem("hid")
    wsrc = {"w1": w1t, "w2": w2t}
    thcol = {"w1": 0, "w2": 1, "w3": 2}
    qtok = {}
    R.wait("sync", spill, 16)
    R.wait("sync", ln2_done)
    R.wait("vector", spill, 16)
    R.wait("gpsimd", spill, 16)
    nld = 0
    mm_pe_after = {}
    for j in range(44):
        for wname in ("w1", "w2"):
            ts = tsD if wname == "w1" else tsP
            for half in range(2):
                if (wname, half) in qtok:
                    R.wait("sync", qbar["vector"], qtok[(wname, half)])
                fld[wname].dma("sync", half, f32bufs[wname][half][:],
                               wsrc[wname][:, j * 2048 + half * 1024: j * 2048 + (half + 1) * 1024])
                nld += 1
                fld[wname].wait("vector", half)
                R.wait("vector", dve, th_ready)
                if 2 * j + half >= 1:
                    R.wait("vector", fq[wname], 2 * j + half)
                col = thcol[wname]
                src = f32bufs[wname][half][:]
                dst = wqbufs[wname][half][:]
                R.op("vector", "tensor_scalar", out=ts[:, 0:1024], in0=src,
                     scalar1=S["thb"][:, col:col + 1], scalar2=None,
                     op0=ALU.is_ge).inc(qbar["vector"], 1)
                R.op("vector", "tensor_scalar", out=ts[:, 1024:2048], in0=src,
                     scalar1=S["nthb"][:, col:col + 1], scalar2=None,
                     op0=ALU.is_le).inc(qbar["vector"], 1)
                qbar_n["vector"] += 2
                qtok[(wname, half)] = qbar_n["vector"]
                R.wait("gpsimd", qbar["vector"], qbar_n["vector"])
                if j >= 1 and (j - 1, wname) in mm_pe_after:
                    R.wait("gpsimd", pe, mm_pe_after[(j - 1, wname)])
                R.op("gpsimd", "tensor_tensor", out=dst, in0=ts[:, 0:1024],
                     in1=ts[:, 1024:2048], op=ALU.subtract).inc(fq[wname], 1)
        R.wait("tensor", fq["w1"], 2 * (j + 1))
        R.wait("tensor", fq["w2"], 2 * (j + 1))
        x1banks = ("b0", "b1") if j % 2 == 0 else ("b2", "b3")
        for tc_ in range(2):
            bn = x1banks[tc_]
            BK[bn].acquire()
            for kc in range(16):
                wb = wqbufs["w1"][kc // 8]
                off = (kc % 8) * 128
                o = R.op("tensor", "matmul", BK[bn].t[:, 0:512],
                         wb[:, off:off + 128],
                         AR_A[:, kc * NT + tc_ * 512: kc * NT + tc_ * 512 + 512],
                         start=(kc == 0), stop=(kc == 15))
            o.inc(pe, 1)
            if tc_ == 1:
                mm_pe_after[(j, "w1")] = pe.v
            bn2 = ("b4", "b5")[tc_]
            BK[bn2].acquire()
            for kc in range(16):
                wb = wqbufs["w2"][kc // 8]
                off = (kc % 8) * 128
                o = R.op("tensor", "matmul", BK[bn2].t[:, 0:512],
                         wb[:, off:off + 128],
                         AR_A[:, kc * NT + tc_ * 512: kc * NT + tc_ * 512 + 512],
                         start=(kc == 0), stop=(kc == 15))
            o.inc(pe, 1)
            if tc_ == 1:
                mm_pe_after[(j, "w2")] = pe.v
            R.wait("scalar", pe)
            R.op("scalar", "activation", out=sgb[tc_][:], in_=BK[bn].t[:, 0:512],
                 func=AF.Sigmoid).inc(sgd, 1)
            BK[bn].release("scalar")
            R.wait("vector", sgd)
            R.op("vector", "tensor_tensor", out=hidden_ap(j, tc_ * 512, 512),
                 in0=BK[bn2].t[:, 0:512], in1=sgb[tc_][:], op=ALU.mult)\
                .inc(hid, 1)
            BK[bn2].release("vector")

    w3f = [SCRF[:, 0:1408], SCRF[:, 1408:2816]]
    w3q = [WQ[:, i * 1408:(i + 1) * 1408] for i in range(4)]
    t3a = SCRB[:, 4096:5504]
    t3b = SCRB[:, 5504:6912]
    x1b = [SCRB[:, 0:1024], SCRB[:, 1024:2048]]
    f3ld = Stream(R, "f3ld", 2)
    f3q = R.sem("f3q")
    x1ld = Stream(R, "x1ld", 2)
    oev = R.sem("oev")
    ost = Stream(R, "ost", 2)
    R.wait("sync", fq["w1"], 88)
    R.wait("sync", fq["w2"], 88)
    R.wait("vector", sgd, 88)
    R.wait("vector", hid, 88)
    nld3 = 0
    qtok3 = {}
    w3_pe_after = {}
    for j in range(16):
        p = j % 2
        if j >= 2:
            R.wait("sync", oev, (j - 1) * 2)
        x1ld.dma("sync", p, x1b[p][:], x1_spill[:, j * NT:(j + 1) * NT])
        for part in range(4):
            g = j * 4 + part
            if g >= 2 and (g - 2) in qtok3:
                R.wait("sync", qbar["vector"], qtok3[g - 2])
            f3ld.dma("sync", part % 2, w3f[part % 2][:],
                     w3t[:, j * 5632 + part * 1408: j * 5632 + (part + 1) * 1408])
            nld3 += 1
            f3ld.wait("vector", part % 2)
            R.wait("vector", dve, th_ready)
            col = thcol["w3"]
            if g >= 1:
                R.wait("vector", f3q, g)
            R.op("vector", "tensor_scalar", out=t3a[:], in0=w3f[part % 2][:],
                 scalar1=S["thb"][:, col:col + 1], scalar2=None, op0=ALU.is_ge)\
                .inc(qbar["vector"], 1)
            R.op("vector", "tensor_scalar", out=t3b[:], in0=w3f[part % 2][:],
                 scalar1=S["nthb"][:, col:col + 1], scalar2=None, op0=ALU.is_le)\
                .inc(qbar["vector"], 1)
            qbar_n["vector"] += 2
            qtok3[g] = qbar_n["vector"]
            R.wait("gpsimd", qbar["vector"], qbar_n["vector"])
            if j >= 1 and (j - 1) in w3_pe_after:
                R.wait("gpsimd", pe, w3_pe_after[j - 1])
            R.op("gpsimd", "tensor_tensor", out=w3q[part][:], in0=t3a[:],
                 in1=t3b[:], op=ALU.subtract).inc(f3q, 1)
        R.wait("tensor", f3q, (j + 1) * 4)
        if j == 0:
            R.wait("tensor", hid, 88)
        pn3 = ("b0", "b1") if j % 2 == 0 else ("b2", "b3")
        for tc_ in range(2):
            bn = pn3[tc_]
            BK[bn].acquire()
            for kc in range(44):
                part = kc // 11
                off = (kc % 11) * 128
                o = R.op("tensor", "matmul", BK[bn].t[:, 0:512],
                         w3q[part][:, off:off + 128],
                         hidden_ap(kc, tc_ * 512, 512),
                         start=(kc == 0), stop=(kc == 43))
            o.inc(pe, 1)
            R.wait("vector", pe)
            x1ld.wait("vector", p)
            if j >= 1:
                ost.wait("vector", tc_, j)
            R.op("vector", "tensor_tensor", out=OSTG[tc_][:],
                 in0=BK[bn].t[:, 0:512],
                 in1=x1b[p][:, tc_ * 512: tc_ * 512 + 512],
                 op=ALU.add).inc(oev, 1)
            BK[bn].release("vector")
            R.wait("scalar", oev)
            ost.dma("scalar", tc_,
                    out_c[j * 128:(j + 1) * 128, tc_ * 512:(tc_ + 1) * 512],
                    OSTG[tc_][:])
        w3_pe_after[j] = pe.v
    ost.wait_all("scalar")
    R.replay()
    return nc



_CACHE = {}


def _tile_wT(w, oc):
    t = w.reshape(oc, 128, 16, 128).transpose(3, 0, 2, 1)
    return np.ascontiguousarray(t.reshape(128, oc * 16 * 128))


def _tile_w3T(w3):
    t = w3.reshape(16, 128, 44, 128).transpose(3, 0, 2, 1)
    return np.ascontiguousarray(t.reshape(128, 16 * 44 * 128))


def kernel(L0_attn_x, L0_attn_slope_rate, L0_attn_W_qkvu, L0_attn_W_out,
           L0_ffn_w1, L0_ffn_w2, L0_ffn_w3):
    x = np.asarray(L0_attn_x, dtype=np.float32).reshape(8192, DM)
    slope = np.ascontiguousarray(
        np.asarray(L0_attn_slope_rate, dtype=np.float32).reshape(1, NH))
    Wq = np.asarray(L0_attn_W_qkvu, dtype=np.float32)
    Wo = np.asarray(L0_attn_W_out, dtype=np.float32)
    w1 = np.asarray(L0_ffn_w1, dtype=np.float32)
    w2 = np.asarray(L0_ffn_w2, dtype=np.float32)
    w3 = np.asarray(L0_ffn_w3, dtype=np.float32)

    xT = np.ascontiguousarray(x.T)
    wqkvu_t = _tile_wT(Wq, 64)
    wout_t = _tile_wT(Wo, 16)
    w1_t = _tile_wT(w1, 44)
    w2_t = _tile_wT(w2, 44)
    w3_t = _tile_w3T(w3)

    consts = make_consts()
    if "A" not in _CACHE:
        _CACHE["A"] = build_prog_A(consts)
        _CACHE["B"] = build_prog_B(consts)
    ncA, ncB = _CACHE["A"], _CACHE["B"]

    in_A = []
    for c in range(8):
        in_A.append({
            "xT": np.ascontiguousarray(xT[:, c * NT:(c + 1) * NT]),
            "slope": slope,
            "wqkvu": wqkvu_t,
            "w1s": np.ascontiguousarray(w1_t[:, c * 11264:(c + 1) * 11264]),
            "w2s": np.ascontiguousarray(w2_t[:, c * 11264:(c + 1) * 11264]),
            "w3s": np.ascontiguousarray(w3_t[:, c * 11264:(c + 1) * 11264]),
        })
    resA = run_bass_kernel_spmd(ncA, in_A, core_ids=list(range(8)))

    kv_all = np.ascontiguousarray(
        np.concatenate([resA.results[c]["kv_out"] for c in range(8)], axis=0))
    gparts = np.ascontiguousarray(
        np.concatenate([resA.results[c]["gpart"] for c in range(8)], axis=0))

    in_B = []
    for c in range(8):
        cv = np.full((8, 1), -1e9, dtype=np.float32)
        for cp in range(c):
            cv[cp, 0] = -1024.0 * (c - 1 - cp)
        in_B.append({
            "xT": in_A[c]["xT"],
            "slope": slope,
            "cvec": cv,
            "kv_all": kv_all,
            "gparts": gparts,
            "qs_in": np.asarray(resA.results[c]["qs_out"]),
            "o_in": np.asarray(resA.results[c]["o_out"]),
            "u_in": np.asarray(resA.results[c]["u_out"]),
            "wout": wout_t,
            "w1t": w1_t,
            "w2t": w2_t,
            "w3t": w3_t,
        })
    resB = run_bass_kernel_spmd(ncB, in_B, core_ids=list(range(8)))

    out = np.concatenate([resB.results[c]["out_c"] for c in range(8)], axis=1)
    return np.ascontiguousarray(out.T).reshape(1, 8192, DM).astype(np.float32)


# version 04
# speedup vs baseline: 1.0021x; 1.0021x over previous
import numpy as np
import ml_dtypes
import concourse.bass as bass
import concourse.mybir as mybir
from concourse.bass_utils import run_bass_kernel_spmd

F32 = mybir.dt.float32
BF16 = mybir.dt.bfloat16
AF = mybir.ActivationFunctionType
ALU = mybir.AluOpType
BF = ml_dtypes.bfloat16
AX = mybir.AxisListType.X

NT = 1024
DM = 2048
NH = 16
HD = 5632
FC = DM // 128
TC = NT // 128
NBLK = 4
HC = HD // 128
NW_ELEMS = float(HD * DM)

ENGINES = ("sync", "scalar", "vector", "tensor", "gpsimd")


class Sem:
    __slots__ = ("h", "v", "name")

    def __init__(self, h, name):
        self.h, self.v, self.name = h, 0, name


class Op:
    __slots__ = ("meth", "args", "kwargs", "incs")

    def __init__(self, meth, args, kwargs):
        self.meth, self.args, self.kwargs, self.incs = meth, args, kwargs, []

    def inc(self, sem, n):
        assert not self.incs, f"op {self.meth} already has a sync update"
        self.incs.append((sem, n, sem.v + n))
        sem.v += n
        return self

    def tok(self):
        if self.incs:
            s, n, va = self.incs[0]
            return (s, va)
        return None


class Rec:
    def __init__(self, nc):
        self.nc = nc
        self.q = {e: [] for e in ENGINES}
        self._n = 0
        self._lastwait = {}

    def sem(self, name=None):
        self._n += 1
        nm = name or f"s{self._n}"
        return Sem(self.nc.alloc_semaphore(nm), nm)

    def op(self, eng, meth, *args, **kwargs):
        o = Op(meth, args, kwargs)
        self.q[eng].append(o)
        return o

    def wait(self, eng, sem, val=None):
        v = sem.v if val is None else val
        if v <= 0:
            return
        key = (eng, sem.name)
        if self._lastwait.get(key, -1) >= v:
            return
        self._lastwait[key] = v
        self.q[eng].append(Op("wait_ge", (sem.h, v), {}))

    def dma(self, eng, out, in_, sem, **kw):
        return self.op(eng, "dma_start", out=out, in_=in_, **kw).inc(sem, 16)

    @staticmethod
    def _interval(ap):
        ext = 1
        for (step, count) in ap.ap:
            ext += abs(int(step)) * (int(count) - 1)
        off = int(ap.offset)
        return (ap.tensor.name, off, off + ext)

    def _rw_of(self, o):
        if o.meth in ("wait_ge", "nop"):
            return [], []
        writes, reads = [], []
        args = list(o.args)
        kw = dict(o.kwargs)
        if o.meth == "memset":
            writes.append(self._interval(args[0]))
            return writes, reads
        if "out" in kw:
            writes.append(self._interval(kw.pop("out")))
        for v in list(args) + list(kw.values()):
            if isinstance(v, bass.AP):
                reads.append(self._interval(v))
        return writes, reads

    def _auto_chain(self):
        def overlap(a, b):
            return a[0] == b[0] and a[1] < b[2] and b[1] < a[2]

        for eng in ("vector", "scalar", "gpsimd"):
            ops = self.q[eng]
            chain = self.sem(f"chain_{eng}")
            writers = []
            readers = []
            n = len(ops)
            deps = [None] * n
            is_dma = [o.meth == "dma_start" for o in ops]
            for i, o in enumerate(ops):
                ws, rs = self._rw_of(o)
                need = set()
                for r in rs:
                    for (w, j) in writers:
                        if (not is_dma[j]) and overlap(r, w):
                            need.add(j)
                for w in ws:
                    for (pw, j) in writers:
                        if (not is_dma[j]) and overlap(w, pw):
                            need.add(j)
                    for (pr, j) in readers:
                        if (not is_dma[j]) and overlap(w, pr):
                            need.add(j)
                deps[i] = need
                for w in ws:
                    writers.append((w, i))
                for r in rs:
                    readers.append((r, i))
            producers = set()
            for need in deps:
                producers |= need
            chain_val = {}
            cv = 0
            for j in sorted(producers):
                if not ops[j].incs:
                    cv += 1
                    ops[j].inc(chain, 1)
                    chain_val[j] = cv
                else:
                    pass
            new_ops = []
            lastw = {}
            for i, o in enumerate(ops):
                if deps[i]:
                    for j in sorted(deps[i]):
                        t = ops[j].tok()
                        s, v = t
                        if lastw.get(s.name, -1) < v:
                            new_ops.append(Op("wait_ge", (s.h, v), {}))
                            lastw[s.name] = v
                new_ops.append(o)
            self.q[eng] = new_ops

    def replay(self):
        self._auto_chain()
        with self.nc.Block() as block:
            for eng in ENGINES:
                ops = self.q[eng]

                def runner(engine, ops=ops):
                    for o in ops:
                        r = getattr(engine, o.meth)(*o.args, **o.kwargs)
                        if o.incs:
                            s, nn_, _ = o.incs[0]
                            r.then_inc(s.h, nn_)

                getattr(block, eng)(runner)


class Bank:

    def __init__(self, R, t, name):
        self.R, self.t, self.name = R, t, name
        self.sem = R.sem(f"bk_{name}")
        self.tok = None

    def acquire(self, eng="tensor"):
        if self.tok is not None:
            self.R.wait(eng, self.tok[0], self.tok[1])

    def release(self, eng):
        o = self.R.q[eng][-1]
        if not o.incs:
            o.inc(self.sem, 1)
        self.tok = o.tok()


class Stream:

    def __init__(self, R, name, nslots):
        self.R = R
        self.sems = [R.sem(f"{name}_{i}") for i in range(nslots)]
        self.uses = [0] * nslots

    def dma(self, eng, slot, out, in_, **kw):
        self.uses[slot] += 1
        return self.R.op(eng, "dma_start", out=out, in_=in_, **kw)\
            .inc(self.sems[slot], 16)

    def wait(self, eng, slot, use=None):
        u = self.uses[slot] if use is None else use
        self.R.wait(eng, self.sems[slot], 16 * u)

    def wait_all(self, eng):
        for i, s in enumerate(self.sems):
            self.R.wait(eng, s, 16 * self.uses[i])


def make_consts():
    c = {}
    idx = []
    for k in range(2):
        s = 128 * k + np.arange(128)[:, None]
        m = np.arange(256)[None, :]
        d = (m - s).astype(np.float32)
        d[m < s] = 1e9
        idx.append(np.ascontiguousarray(d))
    c["IDX0"], c["IDX1"] = idx
    c["QDROW"] = np.broadcast_to(
        (np.arange(256, dtype=np.float32) + 1.0)[None, :], (128, 256)).copy()
    kd = np.zeros((128, 2), dtype=np.float32)
    kd[:, 0] = 255.0 - np.arange(128)
    kd[:, 1] = 255.0 - (128 + np.arange(128))
    c["KD2"] = kd
    c["ONESR_F"] = np.ones((1, 128), dtype=np.float32)
    c["ONESC_F"] = np.ones((128, 1), dtype=np.float32)
    c["IDEN_F"] = np.eye(128, dtype=np.float32)
    cc = np.zeros((128, 8), dtype=np.float32)
    cc[:, 0] = 1e-5
    cc[:, 1] = 1e-6
    cc[:, 2] = 1e-8
    cc[:, 3] = 1.0 / 2048.0
    cc[:, 4] = 256.0
    c["CCOL"] = cc
    c["CROW"] = np.ascontiguousarray(cc[0:1, :])
    c["IVEC"] = np.array([[0.0], [-256.0], [-512.0], [-768.0]], dtype=np.float32)
    return c


def load_consts(nc, R, names, consts, sem):
    t = {}
    buf = {}
    for k in names:
        arr = consts[k]
        t[k] = nc.inline_tensor(arr, name=f"c_{k}")
        buf[k] = nc.alloc_sbuf_tensor(f"cb_{k}", list(arr.shape), F32)
        R.dma("sync", buf[k][:], t[k].ap(), sem)
    if "ONESC_F" in buf:
        buf["ONESC_BF"] = nc.alloc_sbuf_tensor("cb_ONESC_BF", [128, 1], BF16)
    if "IDEN_F" in buf:
        buf["IDEN_BF"] = nc.alloc_sbuf_tensor("cb_IDEN_BF", [128, 128], BF16)
    return buf


def finish_consts(nc, R, buf, sem):
    cbf = R.sem("cbf")
    ncast = 0
    R.wait("vector", sem)
    if "ONESC_BF" in buf:
        R.op("vector", "tensor_copy", out=buf["ONESC_BF"][:],
             in_=buf["ONESC_F"][:]).inc(cbf, 1)
        ncast += 1
    if "IDEN_BF" in buf:
        R.op("vector", "tensor_copy", out=buf["IDEN_BF"][:],
             in_=buf["IDEN_F"][:]).inc(cbf, 1)
        ncast += 1
    R.wait("tensor", cbf, ncast)


def emit_layernorm(R, S, BK, src, dst, eps_row, src_sem, tag):
    pe, dve, act = R.s["pe"], R.s["dve"], R.s["act"]
    xsq = [S["scrB"][:, 0:1024], S["scrB"][:, 1024:2048]]
    ssum = [BK["b2"].t[0:1, 0:512], BK["b3"].t[0:1, 0:512]]
    ssq = [BK["b4"].t[0:1, 0:512], BK["b5"].t[0:1, 0:512]]
    for b in ("b2", "b3", "b4", "b5"):
        BK[b].acquire()
    R.wait("scalar", src_sem)
    R.wait("tensor", src_sem)
    sq_done = R.sem(f"lnsq{tag}")
    for fc in range(FC):
        srcc = src[:, fc * NT:(fc + 1) * NT]
        p = fc % 2
        if fc >= 2:
            R.wait("scalar", pe)
        R.op("scalar", "activation", out=xsq[p][:], in_=srcc, func=AF.Square)\
            .inc(sq_done, 1)
        R.wait("tensor", sq_done)
        for half in range(2):
            rsrc = src[:, fc * NT + half * 512: fc * NT + half * 512 + 512]
            R.op("tensor", "matmul", ssum[half], S["ONESC_BF"][:], rsrc,
                 start=(fc == 0), stop=(fc == FC - 1))
            o = R.op("tensor", "matmul", ssq[half], S["ONESC_BF"][:],
                     xsq[p][:, half * 512: half * 512 + 512],
                     start=(fc == 0), stop=(fc == FC - 1))
        o.inc(pe, 1)
    mean, ex2, tmpr = S["rowA"], S["rowB"], S["rowA2"]
    rstd = ex2
    R.wait("vector", pe)
    for half in range(2):
        sl = slice(half * 512, half * 512 + 512)
        R.op("vector", "tensor_scalar", out=mean[0:1, sl], in0=ssum[half],
             scalar1=1.0 / DM, scalar2=None, op0=ALU.mult)
        R.op("vector", "tensor_scalar", out=ex2[0:1, sl], in0=ssq[half],
             scalar1=1.0 / DM, scalar2=None, op0=ALU.mult)
    o = R.op("vector", "tensor_mul", out=tmpr[0:1, 0:1024],
             in0=mean[0:1, 0:1024], in1=mean[0:1, 0:1024])
    o = R.op("vector", "tensor_sub", out=ex2[0:1, 0:1024],
             in0=ex2[0:1, 0:1024], in1=tmpr[0:1, 0:1024])
    o.inc(dve, 1)
    for b in ("b2", "b3", "b4", "b5"):
        BK[b].release("vector")
    R.wait("scalar", dve)
    R.op("scalar", "activation", out=tmpr[0:1, 0:1024], in_=ex2[0:1, 0:1024],
         func=AF.Sqrt, bias=eps_row).inc(act, 1)
    R.wait("vector", act)
    R.op("vector", "reciprocal", out=rstd[0:1, 0:1024],
         in_=tmpr[0:1, 0:1024]).inc(dve, 1)
    for b in ("b2", "b3", "b4", "b5"):
        BK[b].acquire()
    R.wait("tensor", dve)
    bmap = (("b2", 0, True), ("b4", 1, True), ("b3", 0, False), ("b5", 1, False))
    for (bn, half, is_mean) in bmap:
        sl = slice(half * 512, half * 512 + 512)
        o = R.op("tensor", "matmul", BK[bn].t[:, 0:512], S["ONESR_F"][:],
                 (mean if is_mean else rstd)[0:1, sl], start=True, stop=True)
    o.inc(pe, 1)
    meanb = S["scrB"][:, 0:1024]
    rstdb = S["scrB"][:, 1024:2048]
    R.wait("vector", pe)
    for (bn, half, is_mean) in bmap:
        dstb = (meanb if is_mean else rstdb)
        o = R.op("vector", "tensor_copy",
                 out=dstb[:, half * 512: half * 512 + 512],
                 in_=BK[bn].t[:, 0:512])
    o.inc(dve, 1)
    for b in ("b2", "b3", "b4", "b5"):
        BK[b].release("vector")
    ttmp = S["scrB"][:, 2048:3072]
    done = R.sem(f"lnd{tag}")
    for fc in range(FC):
        srcc = src[:, fc * NT:(fc + 1) * NT]
        dstc = dst[:, fc * NT:(fc + 1) * NT]
        R.op("vector", "tensor_sub", out=ttmp[:], in0=srcc, in1=meanb)
        R.op("vector", "tensor_mul", out=dstc, in0=ttmp[:], in1=rstdb)\
            .inc(done, 1)
    return done


def common_setup(nc, b6_bf16=True):
    R = Rec(nc)
    R.s = {k: R.sem(k) for k in ("pe", "dve", "act", "pool", "st")}
    BK = {}
    for i in range(6):
        BK[f"b{i}"] = Bank(R, nc.alloc_psum_tensor(f"pb{i}", [128, 512], F32), f"b{i}")
    BK["b6"] = Bank(R, nc.alloc_psum_tensor("pb6", [128, 512],
                                            BF16 if b6_bf16 else F32), "b6")
    BK["b7"] = Bank(R, nc.alloc_psum_tensor("pb7", [128, 512], F32), "b7")
    return R, BK


def emit_slope_derived(R, S, BK, ld_const):
    pe, dve = R.s["pe"], R.s["dve"]
    sder = R.sem("sder")
    BK["b0"].acquire()
    R.wait("tensor", ld_const)
    R.op("tensor", "matmul", BK["b0"].t[:, 0:16], S["ONESR_F"][:],
         S["slope"][:], start=True, stop=True).inc(pe, 1)
    R.wait("vector", pe)
    R.op("vector", "tensor_scalar", out=S["sneg"][:], in0=BK["b0"].t[:, 0:16],
         scalar1=-1.0, scalar2=None, op0=ALU.mult).inc(dve, 1)
    BK["b0"].release("vector")
    R.wait("scalar", dve)
    R.op("scalar", "activation", out=S["bd_bc"][:], in_=S["sneg"][:],
         func=AF.Exp, scale=S["CCOL"][:, 4:5]).inc(sder, 1)
    return sder



def build_prog_A(consts):
    nc = bass.Bass("TRN2", target_bir_lowering=False, debug=False)
    xT = nc.dram_tensor("xT", [DM, NT], F32, kind="ExternalInput").ap()
    slope = nc.dram_tensor("slope", [1, NH], F32, kind="ExternalInput").ap()
    wqkvu = nc.dram_tensor("wqkvu", [128, 64 * 2048], F32, kind="ExternalInput").ap()
    w1s = nc.dram_tensor("w1s", [128, 11264], F32, kind="ExternalInput").ap()
    w2s = nc.dram_tensor("w2s", [128, 11264], F32, kind="ExternalInput").ap()
    w3s = nc.dram_tensor("w3s", [128, 11264], F32, kind="ExternalInput").ap()
    kv_out = nc.dram_tensor("kv_out", [128, NH * 128], F32, kind="ExternalOutput").ap()
    gpart = nc.dram_tensor("gpart", [1, 4], F32, kind="ExternalOutput").ap()
    qs_out = nc.dram_tensor("qs_out", [128, NH * NT], BF16, kind="ExternalOutput").ap()
    o_out = nc.dram_tensor("o_out", [128, NH * NT], BF16, kind="ExternalOutput").ap()
    u_out = nc.dram_tensor("u_out", [128, FC * NT], BF16, kind="ExternalOutput").ap()

    R, BK = common_setup(nc)
    s = R.s
    A = nc.alloc_sbuf_tensor
    AR_LN = A("ar_ln", [128, FC * NT], BF16)
    AR_Q = A("ar_q", [128, NH * NT], BF16)
    AR_K = A("ar_k", [128, NH * NT], BF16)
    AR_V = A("ar_v", [128, TC * DM], BF16)
    SCRB = A("scrb", [128, 8192], BF16)
    OSTGA = A("ostga", [128, 2048], BF16)
    SCRF = A("scrf", [128, 2880], F32)
    WG = [A(f"wg{i}", [128, 1024], F32) for i in range(2)]
    WGB = [A(f"wgb{i}", [128, 1024], BF16) for i in range(4)]
    KVOUT = A("kvout", [128, NH * 128], F32)
    S = {
        "slope": A("s_slope", [1, NH], F32),
        "sneg": A("s_sneg", [128, NH], F32),
        "bd_bc": A("s_bdbc", [128, NH], F32),
        "rowA": A("s_rowA", [1, 1024], F32),
        "rowA2": A("s_rowA2", [1, 1024], F32),
        "rowB": A("s_rowB", [1, 1024], F32),
        "gacc": A("s_gacc", [128, 4], F32),
        "gtmp": A("s_gtmp", [128, 4], F32),
        "gstg": A("s_gstg", [1, 4], F32),
        "kvbf": A("s_kvbf", [128, 128], BF16),
        "scrB": SCRB,
    }
    ld_const = R.sem("ldconst")
    CB = load_consts(nc, R,
                     ("IDX0", "IDX1", "QDROW", "KD2", "ONESR_F",
                      "ONESC_F", "IDEN_F", "CCOL", "CROW"), consts, ld_const)
    S.update(CB)
    R.dma("sync", S["slope"][:], slope, ld_const)
    finish_consts(nc, R, S, ld_const)
    sder = emit_slope_derived(R, S, BK, ld_const)

    xc = [SCRF[:, 0:1024], SCRF[:, 1024:2048]]
    ldx = Stream(R, "ldx", 2)
    cast_done = R.sem("castdone")
    for fc in range(FC):
        p = fc % 2
        if fc >= 2:
            R.wait("sync", cast_done, fc - 1)
        ldx.dma("sync", p, xc[p][:], xT[fc * 128:(fc + 1) * 128, :])
        ldx.wait("vector", p)
        R.op("vector", "tensor_copy", out=AR_LN[:, fc * NT:(fc + 1) * NT],
             in_=xc[p][:]).inc(cast_done, 1)
    ln1_done = emit_layernorm(R, S, BK, AR_LN, AR_LN, S["CROW"][0:1, 0:1],
                              cast_done, "1")

    wld = Stream(R, "wld", 2)
    wcast = R.sem("wcast")
    sevq = R.sem("sevq")
    sevk = R.sem("sevk")
    vev = R.sem("vev")
    trev = R.sem("trev")
    uev = Stream(R, "uev", 2)
    sevu = R.sem("sevu")
    vstg = [SCRB[:, 3072 + i * 512: 3584 + i * 512] for i in range(2)]
    ustg = [SCRB[:, 4096 + i * 512: 4608 + i * 512] for i in range(2)]
    granule_pe_after = {}
    vtr_pe_after = {}
    for j in range(64):
        for half in range(2):
            g = 2 * j + half
            bgi = g % 4
            if g >= 2:
                R.wait("sync", wcast, g - 1)
            wld.dma("sync", g % 2, WG[g % 2][:],
                    wqkvu[:, j * 2048 + half * 1024: j * 2048 + (half + 1) * 1024])
            wld.wait("vector", g % 2)
            if j >= 2:
                R.wait("vector", s["pe"], granule_pe_after[j - 2])
            R.op("vector", "tensor_copy", out=WGB[bgi][:], in_=WG[g % 2][:])\
                .inc(wcast, 1)
        R.wait("tensor", wcast, 2 * (j + 1))
        if j == 0:
            R.wait("tensor", ln1_done, FC)
        b_lo, b_hi = WGB[(2 * j) % 4], WGB[(2 * j + 1) % 4]
        pnames = ("b0", "b1") if j % 2 == 0 else ("b2", "b3")
        for tc_ in range(2):
            bn = pnames[tc_]
            BK[bn].acquire()
            for kc in range(16):
                wb = b_lo if kc < 8 else b_hi
                off = (kc % 8) * 128
                o = R.op("tensor", "matmul", BK[bn].t[:, 0:512],
                         wb[:, off:off + 128],
                         AR_LN[:, kc * NT + tc_ * 512: kc * NT + tc_ * 512 + 512],
                         start=(kc == 0), stop=(kc == 15))
            o.inc(s["pe"], 1)
            pe_now = s["pe"].v
            if j < 16:
                dst = AR_Q[:, j * NT + tc_ * 512: j * NT + tc_ * 512 + 512]
                R.wait("scalar", s["pe"], pe_now)
                R.op("scalar", "activation", out=dst, in_=BK[bn].t[:, 0:512],
                     func=AF.Silu).inc(sevq, 1)
                BK[bn].release("scalar")
            elif j < 32:
                jj = j - 16
                dst = AR_K[:, jj * NT + tc_ * 512: jj * NT + tc_ * 512 + 512]
                R.wait("scalar", s["pe"], pe_now)
                R.op("scalar", "activation", out=dst, in_=BK[bn].t[:, 0:512],
                     func=AF.Silu).inc(sevk, 1)
                BK[bn].release("scalar")
            elif j < 48:
                jj = j - 32
                R.wait("vector", s["pe"], pe_now)
                key = tc_
                if key in vtr_pe_after:
                    R.wait("vector", s["pe"], vtr_pe_after[key])
                R.op("vector", "tensor_copy", out=vstg[tc_][:],
                     in_=BK[bn].t[:, 0:512]).inc(vev, 1)
                BK[bn].release("vector")
                BK["b6"].acquire()
                R.wait("tensor", vev)
                for i4 in range(4):
                    o = R.op("tensor", "transpose",
                             BK["b6"].t[:, i4 * 128:(i4 + 1) * 128],
                             vstg[tc_][:, i4 * 128:(i4 + 1) * 128],
                             S["IDEN_BF"][:])
                o.inc(s["pe"], 1)
                vtr_pe_after[key] = s["pe"].v
                R.wait("vector", s["pe"], s["pe"].v)
                for i4 in range(4):
                    tch = tc_ * 4 + i4
                    R.op("vector", "tensor_copy",
                         out=AR_V[:, tch * DM + jj * 128: tch * DM + (jj + 1) * 128],
                         in_=BK["b6"].t[:, i4 * 128:(i4 + 1) * 128]).inc(trev, 1)
                BK["b6"].release("vector")
            else:
                jj = j - 48
                un = jj * 2 + tc_
                R.wait("vector", s["pe"], pe_now)
                if un >= 2:
                    uev.wait("vector", un % 2, un // 2)
                R.op("vector", "tensor_copy", out=ustg[un % 2][:],
                     in_=BK[bn].t[:, 0:512]).inc(sevu, 1)
                BK[bn].release("vector")
                R.wait("scalar", sevu)
                uev.dma("scalar", un % 2,
                        u_out[:, jj * NT + tc_ * 512: jj * NT + tc_ * 512 + 512],
                        ustg[un % 2][:])
        granule_pe_after[j] = s["pe"].v

    gb = [SCRF[:, 0:1408], SCRF[:, 1408:2816]]
    gld = Stream(R, "gld", 2)
    gred = R.sem("gred")
    R.wait("gpsimd", cast_done, FC)
    for wi, wsh in enumerate((w1s, w2s, w3s)):
        for kk in range(8):
            nch = wi * 8 + kk
            if nch >= 2:
                R.wait("gpsimd", gred, nch - 1)
            gld.dma("gpsimd", nch % 2, gb[nch % 2][:],
                    wsh[:, kk * 1408:(kk + 1) * 1408])

    gamma_state = {"done": 0, "init": False}
    gbar = R.sem("gbar")

    def emit_gamma_chunks(upto):
        if not gamma_state["init"]:
            R.op("vector", "memset", S["gacc"][:], 0.0)
            gamma_state["init"] = True
        while gamma_state["done"] < min(upto, 24):
            nch = gamma_state["done"]
            wi = nch // 8
            gld.wait("vector", nch % 2, nch // 2 + 1)
            R.op("vector", "tensor_reduce", out=S["gtmp"][:, wi:wi + 1],
                 in_=gb[nch % 2][:], axis=AX, op=ALU.add,
                 apply_absolute_value=True).inc(gbar, 1)
            R.wait("vector", gbar, nch + 1)
            R.op("vector", "tensor_tensor", out=S["gacc"][:, wi:wi + 1],
                 in0=S["gacc"][:, wi:wi + 1], in1=S["gtmp"][:, wi:wi + 1],
                 op=ALU.add).inc(gred, 1)
            gamma_state["done"] += 1

    maskb = [SCRB[:, 5120 + i * 512: 5632 + i * 512] for i in range(2)]
    qdb = [SCRB[:, 6144 + i * 256: 6400 + i * 256] for i in range(2)]
    ktokb = [SCRB[:, 6656 + i * 256: 6912 + i * 256] for i in range(2)]
    qkmb = [SCRB[:, 7168 + i * 512: 7680 + i * 512] for i in range(2)]
    kdb = [SCRF[:, 2816 + 2 * i: 2818 + 2 * i] for i in range(2)]
    smask = R.sem("smask")
    sqs = R.sem("sqs")
    sktok = R.sem("sktok")
    sqkm = R.sem("sqkm")
    skv = R.sem("skv")
    skvbf = R.sem("skvbf")
    soev = R.sem("soev")
    soq = Stream(R, "soq", 2)
    q_total, k_total, v_total = sevq.v, sevk.v, trev.v
    R.wait("vector", s["pe"], granule_pe_after[63])
    R.wait("vector", sder, 1)
    R.wait("tensor", sevq, q_total)
    R.wait("tensor", sevk, k_total)
    R.wait("tensor", trev, v_total)
    R.wait("vector", sevq, q_total)
    C_pe_after = {}
    diag_pe_after = {}
    for h in range(NH):
        hp = h % 2
        R.wait("scalar", sder, 1)
        if h >= 2:
            R.wait("scalar", sqkm, 4 * (h - 1))
            R.wait("scalar", sqs, h - 1)
            R.wait("scalar", sktok, 4 * (h - 1))
        R.op("scalar", "activation", out=maskb[hp][:, 0:256], in_=S["IDX0"][:],
             func=AF.Exp, scale=S["sneg"][:, h:h + 1])
        R.op("scalar", "activation", out=maskb[hp][:, 256:512], in_=S["IDX1"][:],
             func=AF.Exp, scale=S["sneg"][:, h:h + 1])
        R.op("scalar", "activation", out=qdb[hp][:], in_=S["QDROW"][:],
             func=AF.Exp, scale=S["sneg"][:, h:h + 1])
        R.op("scalar", "activation", out=kdb[hp][:], in_=S["KD2"][:],
             func=AF.Exp, scale=S["sneg"][:, h:h + 1]).inc(smask, 1)
        R.wait("vector", smask, h + 1)
        qd = qdb[hp]
        qrep = bass.AP(qd.tensor, qd.offset, [[qd.ap[0][0], 128], [0, 4], [1, 256]])
        R.op("vector", "tensor_tensor",
             out=AR_LN[:, h * NT:(h + 1) * NT].rearrange("p (b m) -> p b m", b=4),
             in0=AR_Q[:, h * NT:(h + 1) * NT].rearrange("p (b m) -> p b m", b=4),
             in1=qrep, op=ALU.mult).inc(sqs, 1)
        for i in range(NBLK):
            blk = h * NBLK + i
            BK["b6"].acquire()
            o = R.op("tensor", "transpose", BK["b6"].t[:, 0:128],
                     AR_K[:, h * NT + i * 256: h * NT + i * 256 + 128],
                     S["IDEN_BF"][:])
            o = R.op("tensor", "transpose", BK["b6"].t[:, 128:256],
                     AR_K[:, h * NT + i * 256 + 128: h * NT + i * 256 + 256],
                     S["IDEN_BF"][:])
            o.inc(s["pe"], 1)
            tr_pe = s["pe"].v
            R.wait("vector", s["pe"], tr_pe)
            R.wait("vector", smask, h + 1)
            prevu = blk - 1 if i > 0 else ((h - 2) * NBLK + 3 if h >= 2 else None)
            if prevu is not None and prevu in C_pe_after:
                R.wait("vector", s["pe"], C_pe_after[prevu])
            R.op("vector", "tensor_scalar", out=ktokb[hp][:, 0:128],
                 in0=BK["b6"].t[:, 0:128], scalar1=kdb[hp][:, 0:1],
                 scalar2=None, op0=ALU.mult)
            R.op("vector", "tensor_scalar", out=ktokb[hp][:, 128:256],
                 in0=BK["b6"].t[:, 128:256], scalar1=kdb[hp][:, 1:2],
                 scalar2=None, op0=ALU.mult).inc(sktok, 1)
            BK["b6"].release("vector")
            BK["b7"].acquire()
            R.wait("tensor", sktok, blk + 1)
            tch = 2 * i
            R.op("tensor", "matmul", BK["b7"].t[:, 0:128],
                 ktokb[hp][:, 0:128],
                 AR_V[:, tch * DM + h * 128: tch * DM + (h + 1) * 128],
                 start=True, stop=False)
            R.op("tensor", "matmul", BK["b7"].t[:, 0:128],
                 ktokb[hp][:, 128:256],
                 AR_V[:, (tch + 1) * DM + h * 128: (tch + 1) * DM + (h + 1) * 128],
                 start=False, stop=True).inc(s["pe"], 1)
            C_pe_after[blk] = s["pe"].v
            BK["b0"].acquire()
            R.op("tensor", "matmul", BK["b0"].t[:, 0:256],
                 AR_K[:, h * NT + i * 256: h * NT + i * 256 + 128],
                 AR_Q[:, h * NT + i * 256: h * NT + (i + 1) * 256],
                 start=True, stop=True)
            R.op("tensor", "matmul", BK["b0"].t[:, 256:512],
                 AR_K[:, h * NT + i * 256 + 128: h * NT + i * 256 + 256],
                 AR_Q[:, h * NT + i * 256: h * NT + (i + 1) * 256],
                 start=True, stop=True).inc(s["pe"], 1)
            qk_pe = s["pe"].v
            R.wait("vector", s["pe"], qk_pe)
            prevq = blk - 1 if i > 0 else ((h - 2) * NBLK + 3 if h >= 2 else None)
            if prevq is not None and prevq in diag_pe_after:
                R.wait("vector", s["pe"], diag_pe_after[prevq])
            R.op("vector", "tensor_tensor", out=qkmb[hp][:, 0:256],
                 in0=BK["b0"].t[:, 0:256], in1=maskb[hp][:, 0:256], op=ALU.mult)
            R.op("vector", "tensor_tensor", out=qkmb[hp][:, 256:512],
                 in0=BK["b0"].t[:, 256:512], in1=maskb[hp][:, 256:512],
                 op=ALU.mult).inc(sqkm, 1)
            BK["b0"].release("vector")
            ob = "b1" if i % 2 == 0 else "b2"
            BK[ob].acquire()
            if i > 0:
                R.wait("tensor", skvbf, h * (NBLK - 1) + i)
                R.wait("tensor", sqs, h + 1)
                R.op("tensor", "matmul", BK[ob].t[:, 0:256], S["kvbf"][:],
                     AR_LN[:, h * NT + i * 256: h * NT + (i + 1) * 256],
                     start=True, stop=False)
            R.wait("tensor", sqkm, blk + 1)
            R.op("tensor", "matmul", BK[ob].t[:, 0:256],
                 AR_V[:, tch * DM + h * 128: tch * DM + (h + 1) * 128],
                 qkmb[hp][:, 0:256], start=(i == 0), stop=False)
            R.op("tensor", "matmul", BK[ob].t[:, 0:256],
                 AR_V[:, (tch + 1) * DM + h * 128: (tch + 1) * DM + (h + 1) * 128],
                 qkmb[hp][:, 256:512], start=False, stop=True).inc(s["pe"], 1)
            diag_pe_after[blk] = s["pe"].v
            R.wait("vector", s["pe"], s["pe"].v)
            kvslice = KVOUT[:, h * 128:(h + 1) * 128]
            if i == 0:
                R.op("vector", "tensor_copy", out=kvslice,
                     in_=BK["b7"].t[:, 0:128]).inc(skv, 1)
            else:
                R.op("vector", "scalar_tensor_tensor", out=kvslice, in0=kvslice,
                     scalar=S["bd_bc"][:, h:h + 1], in1=BK["b7"].t[:, 0:128],
                     op0=ALU.mult, op1=ALU.add).inc(skv, 1)
            BK["b7"].release("vector")
            if i < NBLK - 1:
                R.op("vector", "tensor_copy", out=S["kvbf"][:], in_=kvslice)\
                    .inc(skvbf, 1)
            if i == 0 and h >= 2:
                soq.wait("vector", hp, h // 2)
            R.op("vector", "tensor_copy",
                 out=OSTGA[:, hp * 1024 + i * 256: hp * 1024 + (i + 1) * 256],
                 in_=BK[ob].t[:, 0:256]).inc(soev, 1)
            BK[ob].release("vector")
            if i == NBLK - 1:
                R.wait("scalar", soev)
                soq.dma("scalar", hp, o_out[:, h * NT:(h + 1) * NT],
                        OSTGA[:, hp * 1024:(hp + 1) * 1024])
                emit_gamma_chunks((h + 1) * 24 // NH)

    emit_gamma_chunks(24)
    BK["b7"].acquire()
    R.wait("tensor", gred)
    R.op("tensor", "matmul", BK["b7"].t[0:1, 0:4], S["ONESC_F"][:],
         S["gacc"][:], start=True, stop=True).inc(s["pe"], 1)
    R.wait("vector", s["pe"])
    R.op("vector", "tensor_copy", out=S["gstg"][:], in_=BK["b7"].t[0:1, 0:4])\
        .inc(s["dve"], 1)
    BK["b7"].release("vector")
    gstg_done = s["dve"].v

    st = s["st"]
    R.wait("scalar", skv)
    R.wait("scalar", sqs)
    R.wait("scalar", s["dve"], gstg_done)
    R.dma("scalar", kv_out, KVOUT[:], st)
    R.dma("scalar", gpart, S["gstg"][:], st)
    R.dma("scalar", qs_out, AR_LN[:], st)
    R.wait("scalar", st)
    soq.wait_all("scalar")
    uev.wait_all("scalar")
    R.replay()
    return nc



def build_prog_B(consts):
    nc = bass.Bass("TRN2", target_bir_lowering=False, debug=False)
    xT = nc.dram_tensor("xT", [DM, NT], F32, kind="ExternalInput").ap()
    slope = nc.dram_tensor("slope", [1, NH], F32, kind="ExternalInput").ap()
    cvec = nc.dram_tensor("cvec", [8, 1], F32, kind="ExternalInput").ap()
    kv_all = nc.dram_tensor("kv_all", [8 * 128, NH * 128], F32, kind="ExternalInput").ap()
    gparts = nc.dram_tensor("gparts", [8, 4], F32, kind="ExternalInput").ap()
    qs_in = nc.dram_tensor("qs_in", [128, NH * NT], BF16, kind="ExternalInput").ap()
    o_in = nc.dram_tensor("o_in", [128, NH * NT], BF16, kind="ExternalInput").ap()
    u_in = nc.dram_tensor("u_in", [128, FC * NT], BF16, kind="ExternalInput").ap()
    wout = nc.dram_tensor("wout", [128, 16 * 2048], F32, kind="ExternalInput").ap()
    w1t = nc.dram_tensor("w1t", [128, 44 * 2048], F32, kind="ExternalInput").ap()
    w2t = nc.dram_tensor("w2t", [128, 44 * 2048], F32, kind="ExternalInput").ap()
    w3t = nc.dram_tensor("w3t", [128, 16 * 5632], F32, kind="ExternalInput").ap()
    out_c = nc.dram_tensor("out_c", [DM, NT], F32, kind="ExternalOutput").ap()
    x1_spill = nc.dram_tensor("x1_spill", [128, FC * NT], BF16, kind="Internal").ap()
    wgt_bounce = nc.dram_tensor("wgt_bounce", [1, 128], F32, kind="Internal").ap()
    bdi_bounce = nc.dram_tensor("bdi_bounce", [1, 64], F32, kind="Internal").ap()

    R, BK = common_setup(nc, b6_bf16=False)
    s = R.s
    A = nc.alloc_sbuf_tensor
    AR_A = A("ar_a", [128, NH * NT], BF16)
    AR_B = A("ar_b", [128, NH * NT], BF16)
    AR_C = A("ar_c", [128, NH * NT], BF16)
    AR_D = A("ar_d", [128, NH * NT], BF16)
    SCRB = A("scrb", [128, 6912], BF16)
    SCRF = A("scrf", [128, 4096], F32)
    WQ = A("wq", [128, 8192], BF16)
    KVIN = A("kvin", [128, NH * 128], F32)
    KVINB = A("kvinb", [128, NH * 128], BF16)
    OSTG = [A(f"ostg{i}", [128, 512], F32) for i in range(2)]
    S = {
        "slope": A("s_slope", [1, NH], F32),
        "sneg": A("s_sneg", [128, NH], F32),
        "bd_bc": A("s_bdbc", [128, NH], F32),
        "cvec": A("s_cvec", [8, 1], F32),
        "wgt": A("s_wgt", [8, 16], F32),
        "wgtrow": A("s_wgtrow", [1, 128], F32),
        "wgtbc": A("s_wgtbc", [128, 128], F32),
        "bdi4": A("s_bdi4", [4, 16], F32),
        "bdirow": A("s_bdirow", [1, 64], F32),
        "bdibc": A("s_bdibc", [128, 64], F32),
        "slope8": A("s_slope8", [8, 16], F32),
        "slope4": A("s_slope4", [4, 16], F32),
        "rowA": A("s_rowA", [1, 1024], F32),
        "rowA2": A("s_rowA2", [1, 1024], F32),
        "rowB": A("s_rowB", [1, 1024], F32),
        "grow": A("s_grow", [8, 4], F32),
        "gsum": A("s_gsum", [1, 4], F32),
        "th": A("s_th", [1, 4], F32),
        "nth": A("s_nth", [1, 4], F32),
        "thb": A("s_thb", [128, 4], F32),
        "nthb": A("s_nthb", [128, 4], F32),
        "scrB": SCRB,
    }
    ld_const = R.sem("ldconst")
    CB = load_consts(nc, R,
                     ("QDROW", "ONESR_F", "ONESC_F", "CCOL",
                      "CROW", "IVEC"), consts, ld_const)
    S.update(CB)
    R.dma("sync", S["slope"][:], slope, ld_const)
    R.dma("sync", S["cvec"][:], cvec, ld_const)
    R.dma("sync", S["grow"][:], gparts, ld_const)
    finish_consts(nc, R, S, ld_const)
    sder = emit_slope_derived(R, S, BK, ld_const)

    ldbig = R.sem("ldbig")
    R.dma("sync", AR_A[:], qs_in, ldbig)
    R.dma("sync", AR_B[:], o_in, ldbig)

    pe, dve, act = s["pe"], s["dve"], s["act"]
    BK["b4"].acquire()
    R.wait("tensor", ld_const)
    R.op("tensor", "matmul", BK["b4"].t[0:8, 0:16], S["ONESR_F"][0:1, 0:8],
         S["slope"][:], start=True, stop=True)
    R.op("tensor", "matmul", BK["b4"].t[0:4, 16:32], S["ONESR_F"][0:1, 0:4],
         S["slope"][:], start=True, stop=True).inc(pe, 1)
    R.wait("vector", pe)
    R.op("vector", "tensor_copy", out=S["slope8"][:], in_=BK["b4"].t[0:8, 0:16])
    R.op("vector", "tensor_copy", out=S["slope4"][:], in_=BK["b4"].t[0:4, 16:32])\
        .inc(dve, 1)
    BK["b4"].release("vector")
    R.wait("scalar", dve)
    R.op("scalar", "activation", out=S["wgt"][:], in_=S["slope8"][:],
         func=AF.Exp, scale=S["cvec"][:])
    R.op("scalar", "activation", out=S["bdi4"][:], in_=S["slope4"][:],
         func=AF.Exp, scale=S["IVEC"][:]).inc(act, 1)
    bounce = R.sem("bounce")
    R.wait("scalar", act)
    R.dma("scalar", wgt_bounce.rearrange("a (b c) -> (a b) c", b=8),
          S["wgt"][:], bounce)
    R.dma("scalar", bdi_bounce.rearrange("a (b c) -> (a b) c", b=4),
          S["bdi4"][:], bounce)
    R.wait("sync", bounce, 32)
    R.dma("sync", S["wgtrow"][:], wgt_bounce, bounce)
    R.dma("sync", S["bdirow"][:], bdi_bounce, bounce)
    BK["b4"].acquire()
    R.wait("tensor", bounce, 64)
    R.op("tensor", "matmul", BK["b4"].t[:, 0:128], S["ONESR_F"][:],
         S["wgtrow"][:], start=True, stop=True)
    R.op("tensor", "matmul", BK["b4"].t[:, 128:192], S["ONESR_F"][:],
         S["bdirow"][:], start=True, stop=True).inc(pe, 1)
    R.wait("vector", pe)
    R.op("vector", "tensor_copy", out=S["wgtbc"][:], in_=BK["b4"].t[:, 0:128])
    R.op("vector", "tensor_copy", out=S["bdibc"][:], in_=BK["b4"].t[:, 128:192])\
        .inc(dve, 1)
    BK["b4"].release("vector")
    wgt_ready = dve.v

    BK["b5"].acquire()
    R.wait("tensor", ld_const)
    R.op("tensor", "matmul", BK["b5"].t[0:1, 0:4], S["ONESC_F"][0:8, 0:1],
         S["grow"][:], start=True, stop=True).inc(pe, 1)
    R.wait("vector", pe)
    R.op("vector", "tensor_copy", out=S["gsum"][:], in_=BK["b5"].t[0:1, 0:4])\
        .inc(dve, 1)
    BK["b5"].release("vector")
    R.op("vector", "tensor_scalar", out=S["th"][:], in0=S["gsum"][:],
         scalar1=1.0 / NW_ELEMS, scalar2=1e-8, op0=ALU.mult, op1=ALU.add)
    R.op("vector", "tensor_scalar", out=S["th"][:], in0=S["th"][:],
         scalar1=0.5, scalar2=None, op0=ALU.mult)
    R.op("vector", "tensor_scalar", out=S["nth"][:], in0=S["th"][:],
         scalar1=-1.0, scalar2=None, op0=ALU.mult).inc(dve, 1)
    BK["b5"].acquire()
    R.wait("tensor", dve)
    R.op("tensor", "matmul", BK["b5"].t[:, 0:4], S["ONESR_F"][:], S["th"][:],
         start=True, stop=True)
    R.op("tensor", "matmul", BK["b5"].t[:, 4:8], S["ONESR_F"][:], S["nth"][:],
         start=True, stop=True).inc(pe, 1)
    R.wait("vector", pe)
    R.op("vector", "tensor_copy", out=S["thb"][:], in_=BK["b5"].t[:, 0:4])
    R.op("vector", "tensor_copy", out=S["nthb"][:], in_=BK["b5"].t[:, 4:8])\
        .inc(dve, 1)
    BK["b5"].release("vector")
    th_ready = dve.v

    kvg = [SCRF[:, 0:2048], SCRF[:, 2048:4096]]
    kld = Stream(R, "kld", 2)
    kused = R.sem("kused")
    R.wait("vector", dve, wgt_ready)
    for cp in range(8):
        p = cp % 2
        if cp >= 2:
            R.wait("sync", kused, cp - 1)
        kld.dma("sync", p, kvg[p][:], kv_all[cp * 128:(cp + 1) * 128, :])
        kld.wait("vector", p)
        for h in range(NH):
            dst = KVIN[:, h * 128:(h + 1) * 128]
            srcs = kvg[p][:, h * 128:(h + 1) * 128]
            wcol = S["wgtbc"][:, cp * 16 + h: cp * 16 + h + 1]
            if cp == 0:
                R.op("vector", "tensor_scalar", out=dst, in0=srcs,
                     scalar1=wcol, scalar2=None, op0=ALU.mult)
            else:
                R.op("vector", "scalar_tensor_tensor", out=dst, in0=srcs,
                     scalar=wcol, in1=dst, op0=ALU.mult, op1=ALU.add)
        R.q["vector"][-1].inc(kused, 1)
    R.op("vector", "tensor_copy", out=KVINB[:], in_=KVIN[:]).inc(dve, 1)
    kvin_ready = dve.v

    scorr = R.sem("scorr")
    R.wait("tensor", ldbig, 32)
    R.wait("tensor", dve, kvin_ready)
    for h in range(NH):
        for i in range(NBLK):
            ob = "b0" if (h * NBLK + i) % 2 == 0 else "b1"
            BK[ob].acquire()
            R.op("tensor", "matmul", BK[ob].t[:, 0:256],
                 KVINB[:, h * 128:(h + 1) * 128],
                 AR_A[:, h * NT + i * 256: h * NT + (i + 1) * 256],
                 start=True, stop=True).inc(pe, 1)
            R.wait("vector", pe)
            R.op("vector", "scalar_tensor_tensor",
                 out=AR_B[:, h * NT + i * 256: h * NT + (i + 1) * 256],
                 in0=BK[ob].t[:, 0:256],
                 scalar=S["bdibc"][:, i * 16 + h: i * 16 + h + 1],
                 in1=AR_B[:, h * NT + i * 256: h * NT + (i + 1) * 256],
                 op0=ALU.mult, op1=ALU.add).inc(scorr, 1)
            BK[ob].release("vector")

    osq = [SCRB[:, 0:1024], SCRB[:, 1024:2048]]
    sq2 = R.sem("sq2")
    R.wait("scalar", scorr)
    BK["b4"].acquire()
    BK["b5"].acquire()
    for fc in range(FC):
        p = fc % 2
        if fc >= 2:
            R.wait("scalar", pe)
        R.op("scalar", "activation", out=osq[p][:],
             in_=AR_B[:, fc * NT:(fc + 1) * NT], func=AF.Square).inc(sq2, 1)
        R.wait("tensor", sq2)
        for half in range(2):
            o = R.op("tensor", "matmul",
                     (BK["b4"] if half == 0 else BK["b5"]).t[0:1, 0:512],
                     S["ONESC_BF"][:], osq[p][:, half * 512: half * 512 + 512],
                     start=(fc == 0), stop=(fc == FC - 1))
        o.inc(pe, 1)
    R.wait("scalar", pe)
    R.op("scalar", "activation", out=S["rowA2"][0:1, 0:512],
         in_=BK["b4"].t[0:1, 0:512], func=AF.Sqrt,
         scale=S["CROW"][0:1, 3:4], bias=S["CROW"][0:1, 1:2])
    R.op("scalar", "activation", out=S["rowA2"][0:1, 512:1024],
         in_=BK["b5"].t[0:1, 0:512], func=AF.Sqrt,
         scale=S["CROW"][0:1, 3:4], bias=S["CROW"][0:1, 1:2]).inc(act, 1)
    BK["b4"].release("scalar")
    BK["b5"].release("scalar")
    R.wait("vector", act)
    R.op("vector", "reciprocal", out=S["rowA"][0:1, 0:1024],
         in_=S["rowA2"][0:1, 0:1024]).inc(dve, 1)
    BK["b4"].acquire()
    BK["b5"].acquire()
    R.wait("tensor", dve)
    R.op("tensor", "matmul", BK["b4"].t[:, 0:512], S["ONESR_F"][:],
         S["rowA"][0:1, 0:512], start=True, stop=True)
    R.op("tensor", "matmul", BK["b5"].t[:, 0:512], S["ONESR_F"][:],
         S["rowA"][0:1, 512:1024], start=True, stop=True).inc(pe, 1)
    rstdb = SCRB[:, 2048:3072]
    R.wait("vector", pe)
    R.op("vector", "tensor_copy", out=rstdb[:, 0:512], in_=BK["b4"].t[:, 0:512])
    R.op("vector", "tensor_copy", out=rstdb[:, 512:1024],
         in_=BK["b5"].t[:, 0:512]).inc(dve, 1)
    BK["b4"].release("vector")
    BK["b5"].release("vector")
    uload = R.sem("uload")
    R.dma("sync", AR_C[:], u_in, uload)
    gdone = R.sem("gdone")
    ttmp = SCRB[:, 3072:4096]
    R.wait("vector", uload, 16)
    for fc in range(FC):
        R.op("vector", "tensor_mul", out=ttmp[:],
             in0=AR_B[:, fc * NT:(fc + 1) * NT], in1=rstdb[:])
        R.op("vector", "tensor_mul", out=AR_D[:, fc * NT:(fc + 1) * NT],
             in0=ttmp[:], in1=AR_C[:, fc * NT:(fc + 1) * NT]).inc(gdone, 1)

    WGf = [SCRF[:, 0:1024], SCRF[:, 1024:2048]]
    WGb = [WQ[:, i * 1024:(i + 1) * 1024] for i in range(4)]
    xch = [SCRF[:, 2048:3072], SCRF[:, 3072:4096]]
    wldB = Stream(R, "wldB", 2)
    wcastB = R.sem("wcastB")
    xld = Stream(R, "xld", 2)
    x1done = R.sem("x1done")
    R.wait("sync", kused, 8)
    granule_pe_after = {}
    for j in range(16):
        p = j % 2
        if j >= 2:
            R.wait("sync", x1done, 2 * (j - 1))
        xld.dma("sync", p, xch[p][:], xT[j * 128:(j + 1) * 128, :])
        for half in range(2):
            g = 2 * j + half
            if g >= 2:
                R.wait("sync", wcastB, g - 1)
            wldB.dma("sync", g % 2, WGf[g % 2][:],
                     wout[:, j * 2048 + half * 1024: j * 2048 + (half + 1) * 1024])
            wldB.wait("vector", g % 2)
            if j >= 2:
                R.wait("vector", pe, granule_pe_after[j - 2])
            R.op("vector", "tensor_copy", out=WGb[g % 4][:], in_=WGf[g % 2][:])\
                .inc(wcastB, 1)
        R.wait("tensor", wcastB, 2 * (j + 1))
        if j == 0:
            R.wait("tensor", gdone, FC)
        b_lo, b_hi = WGb[(2 * j) % 4], WGb[(2 * j + 1) % 4]
        pnames = ("b0", "b1") if j % 2 == 0 else ("b2", "b3")
        for tc_ in range(2):
            bn = pnames[tc_]
            BK[bn].acquire()
            for kc in range(16):
                wb = b_lo if kc < 8 else b_hi
                off = (kc % 8) * 128
                o = R.op("tensor", "matmul", BK[bn].t[:, 0:512],
                         wb[:, off:off + 128],
                         AR_D[:, kc * NT + tc_ * 512: kc * NT + tc_ * 512 + 512],
                         start=(kc == 0), stop=(kc == 15))
            o.inc(pe, 1)
            R.wait("vector", pe)
            xld.wait("vector", p)
            R.op("vector", "tensor_tensor",
                 out=AR_B[:, j * NT + tc_ * 512: j * NT + tc_ * 512 + 512],
                 in0=BK[bn].t[:, 0:512],
                 in1=xch[p][:, tc_ * 512: tc_ * 512 + 512],
                 op=ALU.add).inc(x1done, 1)
            BK[bn].release("vector")
        granule_pe_after[j] = pe.v

    ln2_done = emit_layernorm(R, S, BK, AR_B, AR_A, S["CROW"][0:1, 0:1],
                              x1done, "2")
    spill = R.sem("spill")
    R.wait("scalar", ln2_done)
    R.dma("scalar", x1_spill, AR_B[:], spill)

    def hidden_ap(kc, lo, width):
        if kc < 16:
            return AR_B[:, kc * NT + lo: kc * NT + lo + width]
        if kc < 32:
            return AR_C[:, (kc - 16) * NT + lo: (kc - 16) * NT + lo + width]
        return AR_D[:, (kc - 32) * NT + lo: (kc - 32) * NT + lo + width]

    f32bufs = {"w1": [SCRF[:, 0:1024], SCRF[:, 1024:2048]],
               "w2": [SCRF[:, 2048:3072], SCRF[:, 3072:4096]]}
    wqbufs = {"w1": [[WQ[:, 0:1024], WQ[:, 1024:2048]],
                     [WQ[:, 4096:5120], WQ[:, 5120:6144]]],
              "w2": [[WQ[:, 2048:3072], WQ[:, 3072:4096]],
                     [WQ[:, 6144:7168], WQ[:, 7168:8192]]]}
    tsD = SCRB[:, 0:2048]
    tsP = SCRB[:, 2048:4096]
    sgb = [SCRB[:, 4096 + i * 512: 4608 + i * 512] for i in range(2)]
    fld = {"w1": Stream(R, "fld1", 2), "w2": Stream(R, "fld2", 2)}
    qbar = {"vector": R.sem("qbarD"), "gpsimd": R.sem("qbarP")}
    qbar_n = {"vector": 0, "gpsimd": 0}
    fq = {"w1": R.sem("fq1"), "w2": R.sem("fq2")}
    sgd = R.sem("sgd")
    hid = R.sem("hid")
    wsrc = {"w1": w1t, "w2": w2t}
    thcol = {"w1": 0, "w2": 1, "w3": 2}
    qtok = {}
    R.wait("sync", spill, 16)
    R.wait("sync", ln2_done)
    R.wait("vector", spill, 16)
    R.wait("gpsimd", spill, 16)
    nld = 0
    mm_pe_after = {}
    for j in range(44):
        for wname in ("w1", "w2"):
            ts = tsD if wname == "w1" else tsP
            for half in range(2):
                if (wname, half) in qtok:
                    R.wait("sync", qbar["vector"], qtok[(wname, half)])
                fld[wname].dma("sync", half, f32bufs[wname][half][:],
                               wsrc[wname][:, j * 2048 + half * 1024: j * 2048 + (half + 1) * 1024])
                nld += 1
                fld[wname].wait("vector", half)
                R.wait("vector", dve, th_ready)
                if 2 * j + half >= 1:
                    R.wait("vector", fq[wname], 2 * j + half)
                col = thcol[wname]
                src = f32bufs[wname][half][:]
                dst = wqbufs[wname][j % 2][half][:]
                R.op("vector", "tensor_scalar", out=ts[:, 0:1024], in0=src,
                     scalar1=S["thb"][:, col:col + 1], scalar2=None,
                     op0=ALU.is_ge).inc(qbar["vector"], 1)
                R.op("vector", "tensor_scalar", out=ts[:, 1024:2048], in0=src,
                     scalar1=S["nthb"][:, col:col + 1], scalar2=None,
                     op0=ALU.is_le).inc(qbar["vector"], 1)
                qbar_n["vector"] += 2
                qtok[(wname, half)] = qbar_n["vector"]
                R.wait("gpsimd", qbar["vector"], qbar_n["vector"])
                if j >= 2 and (j - 2, wname) in mm_pe_after:
                    R.wait("gpsimd", pe, mm_pe_after[(j - 2, wname)])
                R.op("gpsimd", "tensor_tensor", out=dst, in0=ts[:, 0:1024],
                     in1=ts[:, 1024:2048], op=ALU.subtract).inc(fq[wname], 1)
        R.wait("tensor", fq["w1"], 2 * (j + 1))
        R.wait("tensor", fq["w2"], 2 * (j + 1))
        x1banks = ("b0", "b1") if j % 2 == 0 else ("b2", "b3")
        for tc_ in range(2):
            bn = x1banks[tc_]
            BK[bn].acquire()
            for kc in range(16):
                wb = wqbufs["w1"][j % 2][kc // 8]
                off = (kc % 8) * 128
                o = R.op("tensor", "matmul", BK[bn].t[:, 0:512],
                         wb[:, off:off + 128],
                         AR_A[:, kc * NT + tc_ * 512: kc * NT + tc_ * 512 + 512],
                         start=(kc == 0), stop=(kc == 15))
            o.inc(pe, 1)
            if tc_ == 1:
                mm_pe_after[(j, "w1")] = pe.v
            bn2 = ("b4", "b5")[tc_] if j % 2 == 0 else ("b6", "b7")[tc_]
            BK[bn2].acquire()
            for kc in range(16):
                wb = wqbufs["w2"][j % 2][kc // 8]
                off = (kc % 8) * 128
                o = R.op("tensor", "matmul", BK[bn2].t[:, 0:512],
                         wb[:, off:off + 128],
                         AR_A[:, kc * NT + tc_ * 512: kc * NT + tc_ * 512 + 512],
                         start=(kc == 0), stop=(kc == 15))
            o.inc(pe, 1)
            if tc_ == 1:
                mm_pe_after[(j, "w2")] = pe.v
            R.wait("scalar", pe)
            R.op("scalar", "activation", out=sgb[tc_][:], in_=BK[bn].t[:, 0:512],
                 func=AF.Sigmoid).inc(sgd, 1)
            BK[bn].release("scalar")
            R.wait("vector", sgd)
            R.op("vector", "tensor_tensor", out=hidden_ap(j, tc_ * 512, 512),
                 in0=BK[bn2].t[:, 0:512], in1=sgb[tc_][:], op=ALU.mult)\
                .inc(hid, 1)
            BK[bn2].release("vector")

    w3f = [SCRF[:, 0:1408], SCRF[:, 1408:2816]]
    w3q = [[AR_A[:, par * 5632 + i * 1408: par * 5632 + (i + 1) * 1408]
            for i in range(4)] for par in range(2)]
    t3a = SCRB[:, 4096:5504]
    t3b = SCRB[:, 5504:6912]
    x1b = [SCRB[:, 0:1024], SCRB[:, 1024:2048]]
    f3ld = Stream(R, "f3ld", 2)
    f3q = R.sem("f3q")
    x1ld = Stream(R, "x1ld", 2)
    oev = R.sem("oev")
    ost = Stream(R, "ost", 2)
    R.wait("sync", fq["w1"], 88)
    R.wait("sync", fq["w2"], 88)
    R.wait("vector", sgd, 88)
    R.wait("vector", hid, 88)
    R.wait("gpsimd", pe)
    nld3 = 0
    qtok3 = {}
    w3_pe_after = {}
    for j in range(16):
        p = j % 2
        if j >= 2:
            R.wait("sync", oev, (j - 1) * 2)
        x1ld.dma("sync", p, x1b[p][:], x1_spill[:, j * NT:(j + 1) * NT])
        for part in range(4):
            g = j * 4 + part
            if g >= 2 and (g - 2) in qtok3:
                R.wait("sync", qbar["vector"], qtok3[g - 2])
            f3ld.dma("sync", part % 2, w3f[part % 2][:],
                     w3t[:, j * 5632 + part * 1408: j * 5632 + (part + 1) * 1408])
            nld3 += 1
            f3ld.wait("vector", part % 2)
            R.wait("vector", dve, th_ready)
            col = thcol["w3"]
            if g >= 1:
                R.wait("vector", f3q, g)
            R.op("vector", "tensor_scalar", out=t3a[:], in0=w3f[part % 2][:],
                 scalar1=S["thb"][:, col:col + 1], scalar2=None, op0=ALU.is_ge)\
                .inc(qbar["vector"], 1)
            R.op("vector", "tensor_scalar", out=t3b[:], in0=w3f[part % 2][:],
                 scalar1=S["nthb"][:, col:col + 1], scalar2=None, op0=ALU.is_le)\
                .inc(qbar["vector"], 1)
            qbar_n["vector"] += 2
            qtok3[g] = qbar_n["vector"]
            R.wait("gpsimd", qbar["vector"], qbar_n["vector"])
            if j >= 2 and (j - 2) in w3_pe_after:
                R.wait("gpsimd", pe, w3_pe_after[j - 2])
            R.op("gpsimd", "tensor_tensor", out=w3q[j % 2][part][:], in0=t3a[:],
                 in1=t3b[:], op=ALU.subtract).inc(f3q, 1)
        R.wait("tensor", f3q, (j + 1) * 4)
        if j == 0:
            R.wait("tensor", hid, 88)
        pn3 = ("b0", "b1") if j % 2 == 0 else ("b2", "b3")
        for tc_ in range(2):
            bn = pn3[tc_]
            BK[bn].acquire()
            for kc in range(44):
                part = kc // 11
                off = (kc % 11) * 128
                o = R.op("tensor", "matmul", BK[bn].t[:, 0:512],
                         w3q[j % 2][part][:, off:off + 128],
                         hidden_ap(kc, tc_ * 512, 512),
                         start=(kc == 0), stop=(kc == 43))
            o.inc(pe, 1)
            R.wait("vector", pe)
            x1ld.wait("vector", p)
            if j >= 1:
                ost.wait("vector", tc_, j)
            R.op("vector", "tensor_tensor", out=OSTG[tc_][:],
                 in0=BK[bn].t[:, 0:512],
                 in1=x1b[p][:, tc_ * 512: tc_ * 512 + 512],
                 op=ALU.add).inc(oev, 1)
            BK[bn].release("vector")
            R.wait("scalar", oev)
            ost.dma("scalar", tc_,
                    out_c[j * 128:(j + 1) * 128, tc_ * 512:(tc_ + 1) * 512],
                    OSTG[tc_][:])
        w3_pe_after[j] = pe.v
    ost.wait_all("scalar")
    R.replay()
    return nc



_CACHE = {}


def _tile_wT(w, oc):
    t = w.reshape(oc, 128, 16, 128).transpose(3, 0, 2, 1)
    return np.ascontiguousarray(t.reshape(128, oc * 16 * 128))


def _tile_w3T(w3):
    t = w3.reshape(16, 128, 44, 128).transpose(3, 0, 2, 1)
    return np.ascontiguousarray(t.reshape(128, 16 * 44 * 128))


def kernel(L0_attn_x, L0_attn_slope_rate, L0_attn_W_qkvu, L0_attn_W_out,
           L0_ffn_w1, L0_ffn_w2, L0_ffn_w3):
    x = np.asarray(L0_attn_x, dtype=np.float32).reshape(8192, DM)
    slope = np.ascontiguousarray(
        np.asarray(L0_attn_slope_rate, dtype=np.float32).reshape(1, NH))
    Wq = np.asarray(L0_attn_W_qkvu, dtype=np.float32)
    Wo = np.asarray(L0_attn_W_out, dtype=np.float32)
    w1 = np.asarray(L0_ffn_w1, dtype=np.float32)
    w2 = np.asarray(L0_ffn_w2, dtype=np.float32)
    w3 = np.asarray(L0_ffn_w3, dtype=np.float32)

    xT = np.ascontiguousarray(x.T)
    wqkvu_t = _tile_wT(Wq, 64)
    wout_t = _tile_wT(Wo, 16)
    w1_t = _tile_wT(w1, 44)
    w2_t = _tile_wT(w2, 44)
    w3_t = _tile_w3T(w3)

    consts = make_consts()
    if "A" not in _CACHE:
        _CACHE["A"] = build_prog_A(consts)
        _CACHE["B"] = build_prog_B(consts)
    ncA, ncB = _CACHE["A"], _CACHE["B"]

    in_A = []
    for c in range(8):
        in_A.append({
            "xT": np.ascontiguousarray(xT[:, c * NT:(c + 1) * NT]),
            "slope": slope,
            "wqkvu": wqkvu_t,
            "w1s": np.ascontiguousarray(w1_t[:, c * 11264:(c + 1) * 11264]),
            "w2s": np.ascontiguousarray(w2_t[:, c * 11264:(c + 1) * 11264]),
            "w3s": np.ascontiguousarray(w3_t[:, c * 11264:(c + 1) * 11264]),
        })
    resA = run_bass_kernel_spmd(ncA, in_A, core_ids=list(range(8)))

    kv_all = np.ascontiguousarray(
        np.concatenate([resA.results[c]["kv_out"] for c in range(8)], axis=0))
    gparts = np.ascontiguousarray(
        np.concatenate([resA.results[c]["gpart"] for c in range(8)], axis=0))

    in_B = []
    for c in range(8):
        cv = np.full((8, 1), -1e9, dtype=np.float32)
        for cp in range(c):
            cv[cp, 0] = -1024.0 * (c - 1 - cp)
        in_B.append({
            "xT": in_A[c]["xT"],
            "slope": slope,
            "cvec": cv,
            "kv_all": kv_all,
            "gparts": gparts,
            "qs_in": np.asarray(resA.results[c]["qs_out"]),
            "o_in": np.asarray(resA.results[c]["o_out"]),
            "u_in": np.asarray(resA.results[c]["u_out"]),
            "wout": wout_t,
            "w1t": w1_t,
            "w2t": w2_t,
            "w3t": w3_t,
        })
    resB = run_bass_kernel_spmd(ncB, in_B, core_ids=list(range(8)))

    out = np.concatenate([resB.results[c]["out_c"] for c in range(8)], axis=1)
    return np.ascontiguousarray(out.T).reshape(1, 8192, DM).astype(np.float32)


# version 05
# speedup vs baseline: 1.4119x; 1.4089x over previous
import numpy as np
import ml_dtypes
import concourse.bass as bass
import concourse.mybir as mybir
from concourse.bass_utils import run_bass_kernel_spmd

F32 = mybir.dt.float32
BF16 = mybir.dt.bfloat16
AF = mybir.ActivationFunctionType
ALU = mybir.AluOpType
BF = ml_dtypes.bfloat16
AX = mybir.AxisListType.X

NT = 1024
DM = 2048
NH = 16
HD = 5632
FC = DM // 128
TC = NT // 128
NBLK = 4
HC = HD // 128
NW_ELEMS = float(HD * DM)

ENGINES = ("sync", "scalar", "vector", "tensor", "gpsimd")


class Sem:
    __slots__ = ("h", "v", "name")

    def __init__(self, h, name):
        self.h, self.v, self.name = h, 0, name


class Op:
    __slots__ = ("meth", "args", "kwargs", "incs")

    def __init__(self, meth, args, kwargs):
        self.meth, self.args, self.kwargs, self.incs = meth, args, kwargs, []

    def inc(self, sem, n):
        assert not self.incs, f"op {self.meth} already has a sync update"
        self.incs.append((sem, n, sem.v + n))
        sem.v += n
        return self

    def tok(self):
        if self.incs:
            s, n, va = self.incs[0]
            return (s, va)
        return None


class Rec:
    def __init__(self, nc):
        self.nc = nc
        self.q = {e: [] for e in ENGINES}
        self._n = 0
        self._lastwait = {}

    def sem(self, name=None):
        self._n += 1
        nm = name or f"s{self._n}"
        return Sem(self.nc.alloc_semaphore(nm), nm)

    def op(self, eng, meth, *args, **kwargs):
        o = Op(meth, args, kwargs)
        self.q[eng].append(o)
        return o

    def wait(self, eng, sem, val=None):
        v = sem.v if val is None else val
        if v <= 0:
            return
        key = (eng, sem.name)
        if self._lastwait.get(key, -1) >= v:
            return
        self._lastwait[key] = v
        self.q[eng].append(Op("wait_ge", (sem.h, v), {}))

    def dma(self, eng, out, in_, sem, **kw):
        return self.op(eng, "dma_start", out=out, in_=in_, **kw).inc(sem, 16)

    @staticmethod
    def _interval(ap):
        ext = 1
        for (step, count) in ap.ap:
            ext += abs(int(step)) * (int(count) - 1)
        off = int(ap.offset)
        return (ap.tensor.name, off, off + ext)

    def _rw_of(self, o):
        if o.meth in ("wait_ge", "nop"):
            return [], []
        writes, reads = [], []
        args = list(o.args)
        kw = dict(o.kwargs)
        if o.meth == "memset":
            writes.append(self._interval(args[0]))
            return writes, reads
        if "out" in kw:
            writes.append(self._interval(kw.pop("out")))
        for v in list(args) + list(kw.values()):
            if isinstance(v, bass.AP):
                reads.append(self._interval(v))
        return writes, reads

    def _auto_chain(self):
        def overlap(a, b):
            return a[0] == b[0] and a[1] < b[2] and b[1] < a[2]

        for eng in ("vector", "scalar", "gpsimd"):
            ops = self.q[eng]
            chain = self.sem(f"chain_{eng}")
            writers = []
            readers = []
            n = len(ops)
            deps = [None] * n
            is_dma = [o.meth == "dma_start" for o in ops]
            for i, o in enumerate(ops):
                ws, rs = self._rw_of(o)
                need = set()
                for r in rs:
                    for (w, j) in writers:
                        if (not is_dma[j]) and overlap(r, w):
                            need.add(j)
                for w in ws:
                    for (pw, j) in writers:
                        if (not is_dma[j]) and overlap(w, pw):
                            need.add(j)
                    for (pr, j) in readers:
                        if (not is_dma[j]) and overlap(w, pr):
                            need.add(j)
                deps[i] = need
                for w in ws:
                    writers.append((w, i))
                for r in rs:
                    readers.append((r, i))
            producers = set()
            for need in deps:
                producers |= need
            chain_val = {}
            cv = 0
            for j in sorted(producers):
                if not ops[j].incs:
                    cv += 1
                    ops[j].inc(chain, 1)
                    chain_val[j] = cv
                else:
                    pass
            new_ops = []
            lastw = {}
            for i, o in enumerate(ops):
                if deps[i]:
                    for j in sorted(deps[i]):
                        t = ops[j].tok()
                        s, v = t
                        if lastw.get(s.name, -1) < v:
                            new_ops.append(Op("wait_ge", (s.h, v), {}))
                            lastw[s.name] = v
                new_ops.append(o)
            self.q[eng] = new_ops

    def replay(self):
        self._auto_chain()
        with self.nc.Block() as block:
            for eng in ENGINES:
                ops = self.q[eng]

                def runner(engine, ops=ops):
                    for o in ops:
                        r = getattr(engine, o.meth)(*o.args, **o.kwargs)
                        if o.incs:
                            s, nn_, _ = o.incs[0]
                            r.then_inc(s.h, nn_)

                getattr(block, eng)(runner)


class Bank:

    def __init__(self, R, t, name):
        self.R, self.t, self.name = R, t, name
        self.sem = R.sem(f"bk_{name}")
        self.tok = None

    def acquire(self, eng="tensor"):
        if self.tok is not None:
            self.R.wait(eng, self.tok[0], self.tok[1])

    def release(self, eng):
        o = self.R.q[eng][-1]
        if not o.incs:
            o.inc(self.sem, 1)
        self.tok = o.tok()


class Stream:

    def __init__(self, R, name, nslots):
        self.R = R
        self.sems = [R.sem(f"{name}_{i}") for i in range(nslots)]
        self.uses = [0] * nslots

    def dma(self, eng, slot, out, in_, **kw):
        self.uses[slot] += 1
        return self.R.op(eng, "dma_start", out=out, in_=in_, **kw)\
            .inc(self.sems[slot], 16)

    def wait(self, eng, slot, use=None):
        u = self.uses[slot] if use is None else use
        self.R.wait(eng, self.sems[slot], 16 * u)

    def wait_all(self, eng):
        for i, s in enumerate(self.sems):
            self.R.wait(eng, s, 16 * self.uses[i])


def make_consts():
    c = {}
    idx = []
    for k in range(2):
        s = 128 * k + np.arange(128)[:, None]
        m = np.arange(256)[None, :]
        d = (m - s).astype(np.float32)
        d[m < s] = 1e9
        idx.append(np.ascontiguousarray(d))
    c["IDX0"], c["IDX1"] = idx
    c["QDROW"] = np.broadcast_to(
        (np.arange(256, dtype=np.float32) + 1.0)[None, :], (128, 256)).copy()
    kd = np.zeros((128, 2), dtype=np.float32)
    kd[:, 0] = 255.0 - np.arange(128)
    kd[:, 1] = 255.0 - (128 + np.arange(128))
    c["KD2"] = kd
    c["ONESR_F"] = np.ones((1, 128), dtype=np.float32)
    c["ONESC_F"] = np.ones((128, 1), dtype=np.float32)
    c["IDEN_F"] = np.eye(128, dtype=np.float32)
    cc = np.zeros((128, 8), dtype=np.float32)
    cc[:, 0] = 1e-5
    cc[:, 1] = 1e-6
    cc[:, 2] = 1e-8
    cc[:, 3] = 1.0 / 2048.0
    cc[:, 4] = 256.0
    c["CCOL"] = cc
    c["CROW"] = np.ascontiguousarray(cc[0:1, :])
    c["IVEC"] = np.array([[0.0], [-256.0], [-512.0], [-768.0]], dtype=np.float32)
    return c


def load_consts(nc, R, names, consts, sem):
    t = {}
    buf = {}
    for k in names:
        arr = consts[k]
        t[k] = nc.inline_tensor(arr, name=f"c_{k}")
        buf[k] = nc.alloc_sbuf_tensor(f"cb_{k}", list(arr.shape), F32)
        R.dma("sync", buf[k][:], t[k].ap(), sem)
    if "ONESC_F" in buf:
        buf["ONESC_BF"] = nc.alloc_sbuf_tensor("cb_ONESC_BF", [128, 1], BF16)
    if "IDEN_F" in buf:
        buf["IDEN_BF"] = nc.alloc_sbuf_tensor("cb_IDEN_BF", [128, 128], BF16)
    return buf


def finish_consts(nc, R, buf, sem):
    cbf = R.sem("cbf")
    ncast = 0
    R.wait("vector", sem)
    if "ONESC_BF" in buf:
        R.op("vector", "tensor_copy", out=buf["ONESC_BF"][:],
             in_=buf["ONESC_F"][:]).inc(cbf, 1)
        ncast += 1
    if "IDEN_BF" in buf:
        R.op("vector", "tensor_copy", out=buf["IDEN_BF"][:],
             in_=buf["IDEN_F"][:]).inc(cbf, 1)
        ncast += 1
    R.wait("tensor", cbf, ncast)


def emit_layernorm(R, S, BK, src, dst, eps_row, src_sem, tag):
    pe, dve, act = R.s["pe"], R.s["dve"], R.s["act"]
    xsq = [S["scrB"][:, 0:1024], S["scrB"][:, 1024:2048]]
    ssum = [BK["b2"].t[0:1, 0:512], BK["b3"].t[0:1, 0:512]]
    ssq = [BK["b4"].t[0:1, 0:512], BK["b5"].t[0:1, 0:512]]
    for b in ("b2", "b3", "b4", "b5"):
        BK[b].acquire()
    R.wait("scalar", src_sem)
    R.wait("tensor", src_sem)
    sq_done = R.sem(f"lnsq{tag}")
    for fc in range(FC):
        srcc = src[:, fc * NT:(fc + 1) * NT]
        p = fc % 2
        if fc >= 2:
            R.wait("scalar", pe)
        R.op("scalar", "activation", out=xsq[p][:], in_=srcc, func=AF.Square)\
            .inc(sq_done, 1)
        R.wait("tensor", sq_done)
        for half in range(2):
            rsrc = src[:, fc * NT + half * 512: fc * NT + half * 512 + 512]
            R.op("tensor", "matmul", ssum[half], S["ONESC_BF"][:], rsrc,
                 start=(fc == 0), stop=(fc == FC - 1))
            o = R.op("tensor", "matmul", ssq[half], S["ONESC_BF"][:],
                     xsq[p][:, half * 512: half * 512 + 512],
                     start=(fc == 0), stop=(fc == FC - 1))
        o.inc(pe, 1)
    mean, ex2, tmpr = S["rowA"], S["rowB"], S["rowA2"]
    rstd = ex2
    R.wait("vector", pe)
    for half in range(2):
        sl = slice(half * 512, half * 512 + 512)
        R.op("vector", "tensor_scalar", out=mean[0:1, sl], in0=ssum[half],
             scalar1=1.0 / DM, scalar2=None, op0=ALU.mult)
        R.op("vector", "tensor_scalar", out=ex2[0:1, sl], in0=ssq[half],
             scalar1=1.0 / DM, scalar2=None, op0=ALU.mult)
    o = R.op("vector", "tensor_mul", out=tmpr[0:1, 0:1024],
             in0=mean[0:1, 0:1024], in1=mean[0:1, 0:1024])
    o = R.op("vector", "tensor_sub", out=ex2[0:1, 0:1024],
             in0=ex2[0:1, 0:1024], in1=tmpr[0:1, 0:1024])
    o.inc(dve, 1)
    for b in ("b2", "b3", "b4", "b5"):
        BK[b].release("vector")
    R.wait("scalar", dve)
    R.op("scalar", "activation", out=tmpr[0:1, 0:1024], in_=ex2[0:1, 0:1024],
         func=AF.Sqrt, bias=eps_row).inc(act, 1)
    R.wait("vector", act)
    R.op("vector", "reciprocal", out=rstd[0:1, 0:1024],
         in_=tmpr[0:1, 0:1024]).inc(dve, 1)
    for b in ("b2", "b3", "b4", "b5"):
        BK[b].acquire()
    R.wait("tensor", dve)
    bmap = (("b2", 0, True), ("b4", 1, True), ("b3", 0, False), ("b5", 1, False))
    for (bn, half, is_mean) in bmap:
        sl = slice(half * 512, half * 512 + 512)
        o = R.op("tensor", "matmul", BK[bn].t[:, 0:512], S["ONESR_F"][:],
                 (mean if is_mean else rstd)[0:1, sl], start=True, stop=True)
    o.inc(pe, 1)
    meanb = S["scrB"][:, 0:1024]
    rstdb = S["scrB"][:, 1024:2048]
    R.wait("vector", pe)
    for (bn, half, is_mean) in bmap:
        dstb = (meanb if is_mean else rstdb)
        o = R.op("vector", "tensor_copy",
                 out=dstb[:, half * 512: half * 512 + 512],
                 in_=BK[bn].t[:, 0:512])
    o.inc(dve, 1)
    for b in ("b2", "b3", "b4", "b5"):
        BK[b].release("vector")
    ttmp = S["scrB"][:, 2048:3072]
    done = R.sem(f"lnd{tag}")
    for fc in range(FC):
        srcc = src[:, fc * NT:(fc + 1) * NT]
        dstc = dst[:, fc * NT:(fc + 1) * NT]
        R.op("vector", "tensor_sub", out=ttmp[:], in0=srcc, in1=meanb)
        R.op("vector", "tensor_mul", out=dstc, in0=ttmp[:], in1=rstdb)\
            .inc(done, 1)
    return done


def common_setup(nc, b6_bf16=True):
    R = Rec(nc)
    R.s = {k: R.sem(k) for k in ("pe", "dve", "act", "pool", "st")}
    BK = {}
    for i in range(6):
        BK[f"b{i}"] = Bank(R, nc.alloc_psum_tensor(f"pb{i}", [128, 512], F32), f"b{i}")
    BK["b6"] = Bank(R, nc.alloc_psum_tensor("pb6", [128, 512],
                                            BF16 if b6_bf16 else F32), "b6")
    BK["b7"] = Bank(R, nc.alloc_psum_tensor("pb7", [128, 512], F32), "b7")
    return R, BK


def emit_slope_derived(R, S, BK, ld_const):
    pe, dve = R.s["pe"], R.s["dve"]
    sder = R.sem("sder")
    BK["b0"].acquire()
    R.wait("tensor", ld_const)
    R.op("tensor", "matmul", BK["b0"].t[:, 0:16], S["ONESR_F"][:],
         S["slope"][:], start=True, stop=True).inc(pe, 1)
    R.wait("vector", pe)
    R.op("vector", "tensor_scalar", out=S["sneg"][:], in0=BK["b0"].t[:, 0:16],
         scalar1=-1.0, scalar2=None, op0=ALU.mult).inc(dve, 1)
    BK["b0"].release("vector")
    R.wait("scalar", dve)
    R.op("scalar", "activation", out=S["bd_bc"][:], in_=S["sneg"][:],
         func=AF.Exp, scale=S["CCOL"][:, 4:5]).inc(sder, 1)
    return sder



def build_prog_A(consts):
    nc = bass.Bass("TRN2", target_bir_lowering=False, debug=False)
    xT = nc.dram_tensor("xT", [DM, NT], F32, kind="ExternalInput").ap()
    slope = nc.dram_tensor("slope", [1, NH], F32, kind="ExternalInput").ap()
    wqkvu = nc.dram_tensor("wqkvu", [128, 64 * 2048], F32, kind="ExternalInput").ap()
    w1s = nc.dram_tensor("w1s", [128, 11264], F32, kind="ExternalInput").ap()
    w2s = nc.dram_tensor("w2s", [128, 11264], F32, kind="ExternalInput").ap()
    w3s = nc.dram_tensor("w3s", [128, 11264], F32, kind="ExternalInput").ap()
    kv_out = nc.dram_tensor("kv_out", [128, NH * 128], F32, kind="ExternalOutput").ap()
    gpart = nc.dram_tensor("gpart", [1, 4], F32, kind="ExternalOutput").ap()
    qs_out = nc.dram_tensor("qs_out", [128, NH * NT], BF16, kind="ExternalOutput").ap()
    o_out = nc.dram_tensor("o_out", [128, NH * NT], BF16, kind="ExternalOutput").ap()
    u_out = nc.dram_tensor("u_out", [128, FC * NT], BF16, kind="ExternalOutput").ap()

    R, BK = common_setup(nc)
    s = R.s
    A = nc.alloc_sbuf_tensor
    AR_LN = A("ar_ln", [128, FC * NT], BF16)
    AR_Q = A("ar_q", [128, NH * NT], BF16)
    AR_K = A("ar_k", [128, NH * NT], BF16)
    AR_V = A("ar_v", [128, TC * DM], BF16)
    SCRB = A("scrb", [128, 8192], BF16)
    OSTGA = A("ostga", [128, 2048], BF16)
    SCRF = A("scrf", [128, 2880], F32)
    WG = [A(f"wg{i}", [128, 1024], F32) for i in range(2)]
    WGB = [A(f"wgb{i}", [128, 1024], BF16) for i in range(4)]
    KVOUT = A("kvout", [128, NH * 128], F32)
    S = {
        "slope": A("s_slope", [1, NH], F32),
        "sneg": A("s_sneg", [128, NH], F32),
        "bd_bc": A("s_bdbc", [128, NH], F32),
        "rowA": A("s_rowA", [1, 1024], F32),
        "rowA2": A("s_rowA2", [1, 1024], F32),
        "rowB": A("s_rowB", [1, 1024], F32),
        "gacc": A("s_gacc", [128, 4], F32),
        "gtmp": A("s_gtmp", [128, 4], F32),
        "gstg": A("s_gstg", [1, 4], F32),
        "kvbf": A("s_kvbf", [128, 128], BF16),
        "scrB": SCRB,
    }
    ld_const = R.sem("ldconst")
    CB = load_consts(nc, R,
                     ("IDX0", "IDX1", "QDROW", "KD2", "ONESR_F",
                      "ONESC_F", "IDEN_F", "CCOL", "CROW"), consts, ld_const)
    S.update(CB)
    R.dma("sync", S["slope"][:], slope, ld_const)
    finish_consts(nc, R, S, ld_const)
    sder = emit_slope_derived(R, S, BK, ld_const)

    xc = [SCRF[:, 0:1024], SCRF[:, 1024:2048]]
    ldx = Stream(R, "ldx", 2)
    cast_done = R.sem("castdone")
    for fc in range(FC):
        p = fc % 2
        if fc >= 2:
            R.wait("sync", cast_done, fc - 1)
        ldx.dma("sync", p, xc[p][:], xT[fc * 128:(fc + 1) * 128, :])
        ldx.wait("vector", p)
        R.op("vector", "tensor_copy", out=AR_LN[:, fc * NT:(fc + 1) * NT],
             in_=xc[p][:]).inc(cast_done, 1)
    ln1_done = emit_layernorm(R, S, BK, AR_LN, AR_LN, S["CROW"][0:1, 0:1],
                              cast_done, "1")

    wld = Stream(R, "wld", 2)
    wcast = R.sem("wcast")
    sevq = R.sem("sevq")
    sevk = R.sem("sevk")
    vev = R.sem("vev")
    trev = R.sem("trev")
    uev = Stream(R, "uev", 2)
    sevu = R.sem("sevu")
    vstg = [SCRB[:, 3072 + i * 512: 3584 + i * 512] for i in range(2)]
    ustg = [SCRB[:, 4096 + i * 512: 4608 + i * 512] for i in range(2)]
    granule_pe_after = {}
    vtr_pe_after = {}

    def emit_wload2(jq):
        for half in range(2):
            g = 2 * jq + half
            bgi = g % 4
            if g >= 2:
                R.wait("sync", wcast, g - 1)
            wld.dma("sync", g % 2, WG[g % 2][:],
                    wqkvu[:, jq * 2048 + half * 1024: jq * 2048 + (half + 1) * 1024])
            wld.wait("vector", g % 2)
            if jq >= 2:
                R.wait("vector", s["pe"], granule_pe_after[jq - 2])
            R.op("vector", "tensor_copy", out=WGB[bgi][:], in_=WG[g % 2][:])\
                .inc(wcast, 1)

    emit_wload2(0)
    for j in range(64):
        if j + 1 < 64:
            emit_wload2(j + 1)
        R.wait("tensor", wcast, 2 * (j + 1))
        if j == 0:
            R.wait("tensor", ln1_done, FC)
        b_lo, b_hi = WGB[(2 * j) % 4], WGB[(2 * j + 1) % 4]
        pnames = ("b0", "b1") if j % 2 == 0 else ("b2", "b3")
        for tc_ in range(2):
            bn = pnames[tc_]
            BK[bn].acquire()
            for kc in range(16):
                wb = b_lo if kc < 8 else b_hi
                off = (kc % 8) * 128
                o = R.op("tensor", "matmul", BK[bn].t[:, 0:512],
                         wb[:, off:off + 128],
                         AR_LN[:, kc * NT + tc_ * 512: kc * NT + tc_ * 512 + 512],
                         start=(kc == 0), stop=(kc == 15))
            o.inc(s["pe"], 1)
            pe_now = s["pe"].v
            if j < 16:
                dst = AR_Q[:, j * NT + tc_ * 512: j * NT + tc_ * 512 + 512]
                R.wait("scalar", s["pe"], pe_now)
                R.op("scalar", "activation", out=dst, in_=BK[bn].t[:, 0:512],
                     func=AF.Silu).inc(sevq, 1)
                BK[bn].release("scalar")
            elif j < 32:
                jj = j - 16
                dst = AR_K[:, jj * NT + tc_ * 512: jj * NT + tc_ * 512 + 512]
                R.wait("scalar", s["pe"], pe_now)
                R.op("scalar", "activation", out=dst, in_=BK[bn].t[:, 0:512],
                     func=AF.Silu).inc(sevk, 1)
                BK[bn].release("scalar")
            elif j < 48:
                jj = j - 32
                R.wait("vector", s["pe"], pe_now)
                key = tc_
                if key in vtr_pe_after:
                    R.wait("vector", s["pe"], vtr_pe_after[key])
                R.op("vector", "tensor_copy", out=vstg[tc_][:],
                     in_=BK[bn].t[:, 0:512]).inc(vev, 1)
                BK[bn].release("vector")
                BK["b6"].acquire()
                R.wait("tensor", vev)
                for i4 in range(4):
                    o = R.op("tensor", "transpose",
                             BK["b6"].t[:, i4 * 128:(i4 + 1) * 128],
                             vstg[tc_][:, i4 * 128:(i4 + 1) * 128],
                             S["IDEN_BF"][:])
                o.inc(s["pe"], 1)
                vtr_pe_after[key] = s["pe"].v
                R.wait("vector", s["pe"], s["pe"].v)
                for i4 in range(4):
                    tch = tc_ * 4 + i4
                    R.op("vector", "tensor_copy",
                         out=AR_V[:, tch * DM + jj * 128: tch * DM + (jj + 1) * 128],
                         in_=BK["b6"].t[:, i4 * 128:(i4 + 1) * 128]).inc(trev, 1)
                BK["b6"].release("vector")
            else:
                jj = j - 48
                un = jj * 2 + tc_
                R.wait("vector", s["pe"], pe_now)
                if un >= 2:
                    uev.wait("vector", un % 2, un // 2)
                R.op("vector", "tensor_copy", out=ustg[un % 2][:],
                     in_=BK[bn].t[:, 0:512]).inc(sevu, 1)
                BK[bn].release("vector")
                R.wait("scalar", sevu)
                uev.dma("scalar", un % 2,
                        u_out[:, jj * NT + tc_ * 512: jj * NT + tc_ * 512 + 512],
                        ustg[un % 2][:])
        granule_pe_after[j] = s["pe"].v

    gb = [SCRF[:, 0:1408], SCRF[:, 1408:2816]]
    gld = Stream(R, "gld", 2)
    gred = R.sem("gred")
    R.wait("gpsimd", cast_done, FC)
    for wi, wsh in enumerate((w1s, w2s, w3s)):
        for kk in range(8):
            nch = wi * 8 + kk
            if nch >= 2:
                R.wait("gpsimd", gred, nch - 1)
            gld.dma("gpsimd", nch % 2, gb[nch % 2][:],
                    wsh[:, kk * 1408:(kk + 1) * 1408])

    gamma_state = {"done": 0, "init": False}
    gbar = R.sem("gbar")

    def emit_gamma_chunks(upto):
        if not gamma_state["init"]:
            R.op("vector", "memset", S["gacc"][:], 0.0)
            gamma_state["init"] = True
        while gamma_state["done"] < min(upto, 24):
            nch = gamma_state["done"]
            wi = nch // 8
            gld.wait("vector", nch % 2, nch // 2 + 1)
            R.op("vector", "tensor_reduce", out=S["gtmp"][:, wi:wi + 1],
                 in_=gb[nch % 2][:], axis=AX, op=ALU.add,
                 apply_absolute_value=True).inc(gbar, 1)
            R.wait("vector", gbar, nch + 1)
            R.op("vector", "tensor_tensor", out=S["gacc"][:, wi:wi + 1],
                 in0=S["gacc"][:, wi:wi + 1], in1=S["gtmp"][:, wi:wi + 1],
                 op=ALU.add).inc(gred, 1)
            gamma_state["done"] += 1

    maskb = [SCRB[:, 5120 + i * 512: 5632 + i * 512] for i in range(2)]
    qdb = [SCRB[:, 6144 + i * 256: 6400 + i * 256] for i in range(2)]
    ktokb = [SCRB[:, 6656 + i * 256: 6912 + i * 256] for i in range(2)]
    qkmb = [SCRB[:, 7168 + i * 512: 7680 + i * 512] for i in range(2)]
    kdb = [SCRF[:, 2816 + 2 * i: 2818 + 2 * i] for i in range(2)]
    smask = R.sem("smask")
    sqs = R.sem("sqs")
    sktok = R.sem("sktok")
    sqkm = R.sem("sqkm")
    skv = R.sem("skv")
    skvbf = R.sem("skvbf")
    soev = R.sem("soev")
    soq = Stream(R, "soq", 2)
    q_total, k_total, v_total = sevq.v, sevk.v, trev.v
    R.wait("vector", s["pe"], granule_pe_after[63])
    R.wait("vector", sder, 1)
    R.wait("tensor", sevq, q_total)
    R.wait("tensor", sevk, k_total)
    R.wait("tensor", trev, v_total)
    R.wait("vector", sevq, q_total)
    C_pe_after = {}
    diag_pe_after = {}
    for h in range(NH):
        hp = h % 2
        R.wait("scalar", sder, 1)
        if h >= 2:
            R.wait("scalar", sqkm, 4 * (h - 1))
            R.wait("scalar", sqs, h - 1)
            R.wait("scalar", sktok, 4 * (h - 1))
        R.op("scalar", "activation", out=maskb[hp][:, 0:256], in_=S["IDX0"][:],
             func=AF.Exp, scale=S["sneg"][:, h:h + 1])
        R.op("scalar", "activation", out=maskb[hp][:, 256:512], in_=S["IDX1"][:],
             func=AF.Exp, scale=S["sneg"][:, h:h + 1])
        R.op("scalar", "activation", out=qdb[hp][:], in_=S["QDROW"][:],
             func=AF.Exp, scale=S["sneg"][:, h:h + 1])
        R.op("scalar", "activation", out=kdb[hp][:], in_=S["KD2"][:],
             func=AF.Exp, scale=S["sneg"][:, h:h + 1]).inc(smask, 1)
        R.wait("vector", smask, h + 1)
        qd = qdb[hp]
        qrep = bass.AP(qd.tensor, qd.offset, [[qd.ap[0][0], 128], [0, 4], [1, 256]])
        R.op("vector", "tensor_tensor",
             out=AR_LN[:, h * NT:(h + 1) * NT].rearrange("p (b m) -> p b m", b=4),
             in0=AR_Q[:, h * NT:(h + 1) * NT].rearrange("p (b m) -> p b m", b=4),
             in1=qrep, op=ALU.mult).inc(sqs, 1)
        for i in range(NBLK):
            blk = h * NBLK + i
            BK["b6"].acquire()
            o = R.op("tensor", "transpose", BK["b6"].t[:, 0:128],
                     AR_K[:, h * NT + i * 256: h * NT + i * 256 + 128],
                     S["IDEN_BF"][:])
            o = R.op("tensor", "transpose", BK["b6"].t[:, 128:256],
                     AR_K[:, h * NT + i * 256 + 128: h * NT + i * 256 + 256],
                     S["IDEN_BF"][:])
            o.inc(s["pe"], 1)
            tr_pe = s["pe"].v
            R.wait("vector", s["pe"], tr_pe)
            R.wait("vector", smask, h + 1)
            prevu = blk - 1 if i > 0 else ((h - 2) * NBLK + 3 if h >= 2 else None)
            if prevu is not None and prevu in C_pe_after:
                R.wait("vector", s["pe"], C_pe_after[prevu])
            R.op("vector", "tensor_scalar", out=ktokb[hp][:, 0:128],
                 in0=BK["b6"].t[:, 0:128], scalar1=kdb[hp][:, 0:1],
                 scalar2=None, op0=ALU.mult)
            R.op("vector", "tensor_scalar", out=ktokb[hp][:, 128:256],
                 in0=BK["b6"].t[:, 128:256], scalar1=kdb[hp][:, 1:2],
                 scalar2=None, op0=ALU.mult).inc(sktok, 1)
            BK["b6"].release("vector")
            BK["b7"].acquire()
            R.wait("tensor", sktok, blk + 1)
            tch = 2 * i
            R.op("tensor", "matmul", BK["b7"].t[:, 0:128],
                 ktokb[hp][:, 0:128],
                 AR_V[:, tch * DM + h * 128: tch * DM + (h + 1) * 128],
                 start=True, stop=False)
            R.op("tensor", "matmul", BK["b7"].t[:, 0:128],
                 ktokb[hp][:, 128:256],
                 AR_V[:, (tch + 1) * DM + h * 128: (tch + 1) * DM + (h + 1) * 128],
                 start=False, stop=True).inc(s["pe"], 1)
            C_pe_after[blk] = s["pe"].v
            BK["b0"].acquire()
            R.op("tensor", "matmul", BK["b0"].t[:, 0:256],
                 AR_K[:, h * NT + i * 256: h * NT + i * 256 + 128],
                 AR_Q[:, h * NT + i * 256: h * NT + (i + 1) * 256],
                 start=True, stop=True)
            R.op("tensor", "matmul", BK["b0"].t[:, 256:512],
                 AR_K[:, h * NT + i * 256 + 128: h * NT + i * 256 + 256],
                 AR_Q[:, h * NT + i * 256: h * NT + (i + 1) * 256],
                 start=True, stop=True).inc(s["pe"], 1)
            qk_pe = s["pe"].v
            R.wait("vector", s["pe"], qk_pe)
            prevq = blk - 1 if i > 0 else ((h - 2) * NBLK + 3 if h >= 2 else None)
            if prevq is not None and prevq in diag_pe_after:
                R.wait("vector", s["pe"], diag_pe_after[prevq])
            R.op("vector", "tensor_tensor", out=qkmb[hp][:, 0:256],
                 in0=BK["b0"].t[:, 0:256], in1=maskb[hp][:, 0:256], op=ALU.mult)
            R.op("vector", "tensor_tensor", out=qkmb[hp][:, 256:512],
                 in0=BK["b0"].t[:, 256:512], in1=maskb[hp][:, 256:512],
                 op=ALU.mult).inc(sqkm, 1)
            BK["b0"].release("vector")
            ob = "b1" if i % 2 == 0 else "b2"
            BK[ob].acquire()
            if i > 0:
                R.wait("tensor", skvbf, h * (NBLK - 1) + i)
                R.wait("tensor", sqs, h + 1)
                R.op("tensor", "matmul", BK[ob].t[:, 0:256], S["kvbf"][:],
                     AR_LN[:, h * NT + i * 256: h * NT + (i + 1) * 256],
                     start=True, stop=False)
            R.wait("tensor", sqkm, blk + 1)
            R.op("tensor", "matmul", BK[ob].t[:, 0:256],
                 AR_V[:, tch * DM + h * 128: tch * DM + (h + 1) * 128],
                 qkmb[hp][:, 0:256], start=(i == 0), stop=False)
            R.op("tensor", "matmul", BK[ob].t[:, 0:256],
                 AR_V[:, (tch + 1) * DM + h * 128: (tch + 1) * DM + (h + 1) * 128],
                 qkmb[hp][:, 256:512], start=False, stop=True).inc(s["pe"], 1)
            diag_pe_after[blk] = s["pe"].v
            R.wait("vector", s["pe"], s["pe"].v)
            kvslice = KVOUT[:, h * 128:(h + 1) * 128]
            if i == 0:
                R.op("vector", "tensor_copy", out=kvslice,
                     in_=BK["b7"].t[:, 0:128]).inc(skv, 1)
            else:
                R.op("vector", "scalar_tensor_tensor", out=kvslice, in0=kvslice,
                     scalar=S["bd_bc"][:, h:h + 1], in1=BK["b7"].t[:, 0:128],
                     op0=ALU.mult, op1=ALU.add).inc(skv, 1)
            BK["b7"].release("vector")
            if i < NBLK - 1:
                R.op("vector", "tensor_copy", out=S["kvbf"][:], in_=kvslice)\
                    .inc(skvbf, 1)
            if i == 0 and h >= 2:
                soq.wait("vector", hp, h // 2)
            R.op("vector", "tensor_copy",
                 out=OSTGA[:, hp * 1024 + i * 256: hp * 1024 + (i + 1) * 256],
                 in_=BK[ob].t[:, 0:256]).inc(soev, 1)
            BK[ob].release("vector")
            if i == NBLK - 1:
                R.wait("scalar", soev)
                soq.dma("scalar", hp, o_out[:, h * NT:(h + 1) * NT],
                        OSTGA[:, hp * 1024:(hp + 1) * 1024])
                emit_gamma_chunks((h + 1) * 24 // NH)

    emit_gamma_chunks(24)
    BK["b7"].acquire()
    R.wait("tensor", gred)
    R.op("tensor", "matmul", BK["b7"].t[0:1, 0:4], S["ONESC_F"][:],
         S["gacc"][:], start=True, stop=True).inc(s["pe"], 1)
    R.wait("vector", s["pe"])
    R.op("vector", "tensor_copy", out=S["gstg"][:], in_=BK["b7"].t[0:1, 0:4])\
        .inc(s["dve"], 1)
    BK["b7"].release("vector")
    gstg_done = s["dve"].v

    st = s["st"]
    R.wait("scalar", skv)
    R.wait("scalar", sqs)
    R.wait("scalar", s["dve"], gstg_done)
    R.dma("scalar", kv_out, KVOUT[:], st)
    R.dma("scalar", gpart, S["gstg"][:], st)
    R.dma("scalar", qs_out, AR_LN[:], st)
    R.wait("scalar", st)
    soq.wait_all("scalar")
    uev.wait_all("scalar")
    R.replay()
    return nc



def build_prog_B(consts):
    nc = bass.Bass("TRN2", target_bir_lowering=False, debug=False)
    xT = nc.dram_tensor("xT", [DM, NT], F32, kind="ExternalInput").ap()
    slope = nc.dram_tensor("slope", [1, NH], F32, kind="ExternalInput").ap()
    cvec = nc.dram_tensor("cvec", [8, 1], F32, kind="ExternalInput").ap()
    kv_all = nc.dram_tensor("kv_all", [8 * 128, NH * 128], F32, kind="ExternalInput").ap()
    gparts = nc.dram_tensor("gparts", [8, 4], F32, kind="ExternalInput").ap()
    qs_in = nc.dram_tensor("qs_in", [128, NH * NT], BF16, kind="ExternalInput").ap()
    o_in = nc.dram_tensor("o_in", [128, NH * NT], BF16, kind="ExternalInput").ap()
    u_in = nc.dram_tensor("u_in", [128, FC * NT], BF16, kind="ExternalInput").ap()
    wout = nc.dram_tensor("wout", [128, 16 * 2048], F32, kind="ExternalInput").ap()
    w1t = nc.dram_tensor("w1t", [128, 44 * 2048], F32, kind="ExternalInput").ap()
    w2t = nc.dram_tensor("w2t", [128, 44 * 2048], F32, kind="ExternalInput").ap()
    w3t = nc.dram_tensor("w3t", [128, 16 * 5632], F32, kind="ExternalInput").ap()
    out_c = nc.dram_tensor("out_c", [DM, NT], F32, kind="ExternalOutput").ap()
    x1_spill = nc.dram_tensor("x1_spill", [128, FC * NT], BF16, kind="Internal").ap()
    wgt_bounce = nc.dram_tensor("wgt_bounce", [1, 128], F32, kind="Internal").ap()
    bdi_bounce = nc.dram_tensor("bdi_bounce", [1, 64], F32, kind="Internal").ap()

    R, BK = common_setup(nc, b6_bf16=False)
    s = R.s
    A = nc.alloc_sbuf_tensor
    AR_A = A("ar_a", [128, NH * NT], BF16)
    AR_B = A("ar_b", [128, NH * NT], BF16)
    AR_C = A("ar_c", [128, NH * NT], BF16)
    AR_D = A("ar_d", [128, NH * NT], BF16)
    SCRB = A("scrb", [128, 6912], BF16)
    SCRF = A("scrf", [128, 4096], F32)
    WQ = A("wq", [128, 8192], BF16)
    KVIN = A("kvin", [128, NH * 128], F32)
    KVINB = A("kvinb", [128, NH * 128], BF16)
    OSTG = [A(f"ostg{i}", [128, 512], F32) for i in range(2)]
    S = {
        "slope": A("s_slope", [1, NH], F32),
        "sneg": A("s_sneg", [128, NH], F32),
        "bd_bc": A("s_bdbc", [128, NH], F32),
        "cvec": A("s_cvec", [8, 1], F32),
        "wgt": A("s_wgt", [8, 16], F32),
        "wgtrow": A("s_wgtrow", [1, 128], F32),
        "wgtbc": A("s_wgtbc", [128, 128], F32),
        "bdi4": A("s_bdi4", [4, 16], F32),
        "bdirow": A("s_bdirow", [1, 64], F32),
        "bdibc": A("s_bdibc", [128, 64], F32),
        "slope8": A("s_slope8", [8, 16], F32),
        "slope4": A("s_slope4", [4, 16], F32),
        "rowA": A("s_rowA", [1, 1024], F32),
        "rowA2": A("s_rowA2", [1, 1024], F32),
        "rowB": A("s_rowB", [1, 1024], F32),
        "grow": A("s_grow", [8, 4], F32),
        "gsum": A("s_gsum", [1, 4], F32),
        "th": A("s_th", [1, 4], F32),
        "nth": A("s_nth", [1, 4], F32),
        "thb": A("s_thb", [128, 4], F32),
        "nthb": A("s_nthb", [128, 4], F32),
        "scrB": SCRB,
    }
    ld_const = R.sem("ldconst")
    CB = load_consts(nc, R,
                     ("QDROW", "ONESR_F", "ONESC_F", "CCOL",
                      "CROW", "IVEC"), consts, ld_const)
    S.update(CB)
    R.dma("sync", S["slope"][:], slope, ld_const)
    R.dma("sync", S["cvec"][:], cvec, ld_const)
    R.dma("sync", S["grow"][:], gparts, ld_const)
    finish_consts(nc, R, S, ld_const)
    sder = emit_slope_derived(R, S, BK, ld_const)

    ldbig = R.sem("ldbig")
    R.dma("sync", AR_A[:], qs_in, ldbig)
    R.dma("sync", AR_B[:], o_in, ldbig)

    pe, dve, act = s["pe"], s["dve"], s["act"]
    BK["b4"].acquire()
    R.wait("tensor", ld_const)
    R.op("tensor", "matmul", BK["b4"].t[0:8, 0:16], S["ONESR_F"][0:1, 0:8],
         S["slope"][:], start=True, stop=True)
    R.op("tensor", "matmul", BK["b4"].t[0:4, 16:32], S["ONESR_F"][0:1, 0:4],
         S["slope"][:], start=True, stop=True).inc(pe, 1)
    R.wait("vector", pe)
    R.op("vector", "tensor_copy", out=S["slope8"][:], in_=BK["b4"].t[0:8, 0:16])
    R.op("vector", "tensor_copy", out=S["slope4"][:], in_=BK["b4"].t[0:4, 16:32])\
        .inc(dve, 1)
    BK["b4"].release("vector")
    R.wait("scalar", dve)
    R.op("scalar", "activation", out=S["wgt"][:], in_=S["slope8"][:],
         func=AF.Exp, scale=S["cvec"][:])
    R.op("scalar", "activation", out=S["bdi4"][:], in_=S["slope4"][:],
         func=AF.Exp, scale=S["IVEC"][:]).inc(act, 1)
    bounce = R.sem("bounce")
    R.wait("scalar", act)
    R.dma("scalar", wgt_bounce.rearrange("a (b c) -> (a b) c", b=8),
          S["wgt"][:], bounce)
    R.dma("scalar", bdi_bounce.rearrange("a (b c) -> (a b) c", b=4),
          S["bdi4"][:], bounce)
    R.wait("sync", bounce, 32)
    R.dma("sync", S["wgtrow"][:], wgt_bounce, bounce)
    R.dma("sync", S["bdirow"][:], bdi_bounce, bounce)
    BK["b4"].acquire()
    R.wait("tensor", bounce, 64)
    R.op("tensor", "matmul", BK["b4"].t[:, 0:128], S["ONESR_F"][:],
         S["wgtrow"][:], start=True, stop=True)
    R.op("tensor", "matmul", BK["b4"].t[:, 128:192], S["ONESR_F"][:],
         S["bdirow"][:], start=True, stop=True).inc(pe, 1)
    R.wait("vector", pe)
    R.op("vector", "tensor_copy", out=S["wgtbc"][:], in_=BK["b4"].t[:, 0:128])
    R.op("vector", "tensor_copy", out=S["bdibc"][:], in_=BK["b4"].t[:, 128:192])\
        .inc(dve, 1)
    BK["b4"].release("vector")
    wgt_ready = dve.v

    BK["b5"].acquire()
    R.wait("tensor", ld_const)
    R.op("tensor", "matmul", BK["b5"].t[0:1, 0:4], S["ONESC_F"][0:8, 0:1],
         S["grow"][:], start=True, stop=True).inc(pe, 1)
    R.wait("vector", pe)
    R.op("vector", "tensor_copy", out=S["gsum"][:], in_=BK["b5"].t[0:1, 0:4])\
        .inc(dve, 1)
    BK["b5"].release("vector")
    R.op("vector", "tensor_scalar", out=S["th"][:], in0=S["gsum"][:],
         scalar1=1.0 / NW_ELEMS, scalar2=1e-8, op0=ALU.mult, op1=ALU.add)
    R.op("vector", "tensor_scalar", out=S["th"][:], in0=S["th"][:],
         scalar1=0.5, scalar2=None, op0=ALU.mult)
    R.op("vector", "tensor_scalar", out=S["nth"][:], in0=S["th"][:],
         scalar1=-1.0, scalar2=None, op0=ALU.mult).inc(dve, 1)
    BK["b5"].acquire()
    R.wait("tensor", dve)
    R.op("tensor", "matmul", BK["b5"].t[:, 0:4], S["ONESR_F"][:], S["th"][:],
         start=True, stop=True)
    R.op("tensor", "matmul", BK["b5"].t[:, 4:8], S["ONESR_F"][:], S["nth"][:],
         start=True, stop=True).inc(pe, 1)
    R.wait("vector", pe)
    R.op("vector", "tensor_copy", out=S["thb"][:], in_=BK["b5"].t[:, 0:4])
    R.op("vector", "tensor_copy", out=S["nthb"][:], in_=BK["b5"].t[:, 4:8])\
        .inc(dve, 1)
    BK["b5"].release("vector")
    th_ready = dve.v

    kvg = [SCRF[:, 0:2048], SCRF[:, 2048:4096]]
    kld = Stream(R, "kld", 2)
    kused = R.sem("kused")
    R.wait("vector", dve, wgt_ready)
    for cp in range(8):
        p = cp % 2
        if cp >= 2:
            R.wait("sync", kused, cp - 1)
        kld.dma("sync", p, kvg[p][:], kv_all[cp * 128:(cp + 1) * 128, :])
        kld.wait("vector", p)
        for h in range(NH):
            dst = KVIN[:, h * 128:(h + 1) * 128]
            srcs = kvg[p][:, h * 128:(h + 1) * 128]
            wcol = S["wgtbc"][:, cp * 16 + h: cp * 16 + h + 1]
            if cp == 0:
                R.op("vector", "tensor_scalar", out=dst, in0=srcs,
                     scalar1=wcol, scalar2=None, op0=ALU.mult)
            else:
                R.op("vector", "scalar_tensor_tensor", out=dst, in0=srcs,
                     scalar=wcol, in1=dst, op0=ALU.mult, op1=ALU.add)
        R.q["vector"][-1].inc(kused, 1)
    R.op("vector", "tensor_copy", out=KVINB[:], in_=KVIN[:]).inc(dve, 1)
    kvin_ready = dve.v

    scorr = R.sem("scorr")
    R.wait("tensor", ldbig, 32)
    R.wait("tensor", dve, kvin_ready)
    for h in range(NH):
        for i in range(NBLK):
            ob = "b0" if (h * NBLK + i) % 2 == 0 else "b1"
            BK[ob].acquire()
            R.op("tensor", "matmul", BK[ob].t[:, 0:256],
                 KVINB[:, h * 128:(h + 1) * 128],
                 AR_A[:, h * NT + i * 256: h * NT + (i + 1) * 256],
                 start=True, stop=True).inc(pe, 1)
            R.wait("vector", pe)
            R.op("vector", "scalar_tensor_tensor",
                 out=AR_B[:, h * NT + i * 256: h * NT + (i + 1) * 256],
                 in0=BK[ob].t[:, 0:256],
                 scalar=S["bdibc"][:, i * 16 + h: i * 16 + h + 1],
                 in1=AR_B[:, h * NT + i * 256: h * NT + (i + 1) * 256],
                 op0=ALU.mult, op1=ALU.add).inc(scorr, 1)
            BK[ob].release("vector")

    osq = [SCRB[:, 0:1024], SCRB[:, 1024:2048]]
    sq2 = R.sem("sq2")
    R.wait("scalar", scorr)
    BK["b4"].acquire()
    BK["b5"].acquire()
    for fc in range(FC):
        p = fc % 2
        if fc >= 2:
            R.wait("scalar", pe)
        R.op("scalar", "activation", out=osq[p][:],
             in_=AR_B[:, fc * NT:(fc + 1) * NT], func=AF.Square).inc(sq2, 1)
        R.wait("tensor", sq2)
        for half in range(2):
            o = R.op("tensor", "matmul",
                     (BK["b4"] if half == 0 else BK["b5"]).t[0:1, 0:512],
                     S["ONESC_BF"][:], osq[p][:, half * 512: half * 512 + 512],
                     start=(fc == 0), stop=(fc == FC - 1))
        o.inc(pe, 1)
    R.wait("scalar", pe)
    R.op("scalar", "activation", out=S["rowA2"][0:1, 0:512],
         in_=BK["b4"].t[0:1, 0:512], func=AF.Sqrt,
         scale=S["CROW"][0:1, 3:4], bias=S["CROW"][0:1, 1:2])
    R.op("scalar", "activation", out=S["rowA2"][0:1, 512:1024],
         in_=BK["b5"].t[0:1, 0:512], func=AF.Sqrt,
         scale=S["CROW"][0:1, 3:4], bias=S["CROW"][0:1, 1:2]).inc(act, 1)
    BK["b4"].release("scalar")
    BK["b5"].release("scalar")
    R.wait("vector", act)
    R.op("vector", "reciprocal", out=S["rowA"][0:1, 0:1024],
         in_=S["rowA2"][0:1, 0:1024]).inc(dve, 1)
    BK["b4"].acquire()
    BK["b5"].acquire()
    R.wait("tensor", dve)
    R.op("tensor", "matmul", BK["b4"].t[:, 0:512], S["ONESR_F"][:],
         S["rowA"][0:1, 0:512], start=True, stop=True)
    R.op("tensor", "matmul", BK["b5"].t[:, 0:512], S["ONESR_F"][:],
         S["rowA"][0:1, 512:1024], start=True, stop=True).inc(pe, 1)
    rstdb = SCRB[:, 2048:3072]
    R.wait("vector", pe)
    R.op("vector", "tensor_copy", out=rstdb[:, 0:512], in_=BK["b4"].t[:, 0:512])
    R.op("vector", "tensor_copy", out=rstdb[:, 512:1024],
         in_=BK["b5"].t[:, 0:512]).inc(dve, 1)
    BK["b4"].release("vector")
    BK["b5"].release("vector")
    uload = R.sem("uload")
    R.dma("sync", AR_C[:], u_in, uload)
    gdone = R.sem("gdone")
    ttmp = SCRB[:, 3072:4096]
    R.wait("vector", uload, 16)
    for fc in range(FC):
        R.op("vector", "tensor_mul", out=ttmp[:],
             in0=AR_B[:, fc * NT:(fc + 1) * NT], in1=rstdb[:])
        R.op("vector", "tensor_mul", out=AR_D[:, fc * NT:(fc + 1) * NT],
             in0=ttmp[:], in1=AR_C[:, fc * NT:(fc + 1) * NT]).inc(gdone, 1)

    WGf = [SCRF[:, 0:1024], SCRF[:, 1024:2048]]
    WGb = [WQ[:, i * 1024:(i + 1) * 1024] for i in range(4)]
    xch = [SCRF[:, 2048:3072], SCRF[:, 3072:4096]]
    wldB = Stream(R, "wldB", 2)
    wcastB = R.sem("wcastB")
    xld = Stream(R, "xld", 2)
    x1done = R.sem("x1done")
    R.wait("sync", kused, 8)
    granule_pe_after = {}

    def emit_wload4(jq):
        p = jq % 2
        if jq >= 2:
            R.wait("sync", x1done, 2 * (jq - 1))
        xld.dma("sync", p, xch[p][:], xT[jq * 128:(jq + 1) * 128, :])
        for half in range(2):
            g = 2 * jq + half
            if g >= 2:
                R.wait("sync", wcastB, g - 1)
            wldB.dma("sync", g % 2, WGf[g % 2][:],
                     wout[:, jq * 2048 + half * 1024: jq * 2048 + (half + 1) * 1024])
            wldB.wait("vector", g % 2)
            if jq >= 2:
                R.wait("vector", pe, granule_pe_after[jq - 2])
            R.op("vector", "tensor_copy", out=WGb[g % 4][:], in_=WGf[g % 2][:])\
                .inc(wcastB, 1)

    emit_wload4(0)
    for j in range(16):
        p = j % 2
        if j + 1 < 16:
            emit_wload4(j + 1)
        R.wait("tensor", wcastB, 2 * (j + 1))
        if j == 0:
            R.wait("tensor", gdone, FC)
        b_lo, b_hi = WGb[(2 * j) % 4], WGb[(2 * j + 1) % 4]
        pnames = ("b0", "b1") if j % 2 == 0 else ("b2", "b3")
        for tc_ in range(2):
            bn = pnames[tc_]
            BK[bn].acquire()
            for kc in range(16):
                wb = b_lo if kc < 8 else b_hi
                off = (kc % 8) * 128
                o = R.op("tensor", "matmul", BK[bn].t[:, 0:512],
                         wb[:, off:off + 128],
                         AR_D[:, kc * NT + tc_ * 512: kc * NT + tc_ * 512 + 512],
                         start=(kc == 0), stop=(kc == 15))
            o.inc(pe, 1)
            R.wait("vector", pe)
            xld.wait("vector", p)
            R.op("vector", "tensor_tensor",
                 out=AR_B[:, j * NT + tc_ * 512: j * NT + tc_ * 512 + 512],
                 in0=BK[bn].t[:, 0:512],
                 in1=xch[p][:, tc_ * 512: tc_ * 512 + 512],
                 op=ALU.add).inc(x1done, 1)
            BK[bn].release("vector")
        granule_pe_after[j] = pe.v

    ln2_done = emit_layernorm(R, S, BK, AR_B, AR_A, S["CROW"][0:1, 0:1],
                              x1done, "2")
    spill = R.sem("spill")
    R.wait("scalar", ln2_done)
    R.dma("scalar", x1_spill, AR_B[:], spill)

    def hidden_ap(kc, lo, width):
        if kc < 16:
            return AR_B[:, kc * NT + lo: kc * NT + lo + width]
        if kc < 32:
            return AR_C[:, (kc - 16) * NT + lo: (kc - 16) * NT + lo + width]
        return AR_D[:, (kc - 32) * NT + lo: (kc - 32) * NT + lo + width]

    f32bufs = {"w1": [SCRF[:, 0:1024], SCRF[:, 1024:2048]],
               "w2": [SCRF[:, 2048:3072], SCRF[:, 3072:4096]]}
    wqbufs = {"w1": [[WQ[:, 0:1024], WQ[:, 1024:2048]],
                     [WQ[:, 4096:5120], WQ[:, 5120:6144]]],
              "w2": [[WQ[:, 2048:3072], WQ[:, 3072:4096]],
                     [WQ[:, 6144:7168], WQ[:, 7168:8192]]]}
    tsD = SCRB[:, 0:2048]
    tsP = SCRB[:, 2048:4096]
    sgb = [SCRB[:, 4096 + i * 512: 4608 + i * 512] for i in range(4)]
    fld = {"w1": Stream(R, "fld1", 2), "w2": Stream(R, "fld2", 2)}
    qbar = {"vector": R.sem("qbarD"), "gpsimd": R.sem("qbarP")}
    qbar_n = {"vector": 0, "gpsimd": 0}
    fq = {"w1": R.sem("fq1"), "w2": R.sem("fq2")}
    sgd = R.sem("sgd")
    hid = R.sem("hid")
    wsrc = {"w1": w1t, "w2": w2t}
    thcol = {"w1": 0, "w2": 1, "w3": 2}
    qtok = {}
    R.wait("sync", spill, 16)
    R.wait("sync", ln2_done)
    R.wait("vector", spill, 16)
    R.wait("gpsimd", spill, 16)
    nld = 0
    mm_pe_after = {}

    def emit_quant7(jq):
        for wname in ("w1", "w2"):
            ts = tsD if wname == "w1" else tsP
            for half in range(2):
                if (wname, half) in qtok:
                    R.wait("sync", qbar["vector"], qtok[(wname, half)])
                fld[wname].dma("sync", half, f32bufs[wname][half][:],
                               wsrc[wname][:, jq * 2048 + half * 1024: jq * 2048 + (half + 1) * 1024])
                fld[wname].wait("vector", half)
                R.wait("vector", dve, th_ready)
                if 2 * jq + half >= 1:
                    R.wait("vector", fq[wname], 2 * jq + half)
                col = thcol[wname]
                src = f32bufs[wname][half][:]
                dst = wqbufs[wname][jq % 2][half][:]
                R.op("vector", "tensor_scalar", out=ts[:, 0:1024], in0=src,
                     scalar1=S["thb"][:, col:col + 1], scalar2=None,
                     op0=ALU.is_ge).inc(qbar["vector"], 1)
                R.op("vector", "tensor_scalar", out=ts[:, 1024:2048], in0=src,
                     scalar1=S["nthb"][:, col:col + 1], scalar2=None,
                     op0=ALU.is_le).inc(qbar["vector"], 1)
                qbar_n["vector"] += 2
                qtok[(wname, half)] = qbar_n["vector"]
                R.wait("gpsimd", qbar["vector"], qbar_n["vector"])
                if jq >= 2 and (jq - 2, wname) in mm_pe_after:
                    R.wait("gpsimd", pe, mm_pe_after[(jq - 2, wname)])
                R.op("gpsimd", "tensor_tensor", out=dst, in0=ts[:, 0:1024],
                     in1=ts[:, 1024:2048], op=ALU.subtract).inc(fq[wname], 1)

    emit_quant7(0)
    for j in range(44):
        if j + 1 < 44:
            emit_quant7(j + 1)
        R.wait("tensor", fq["w1"], 2 * (j + 1))
        R.wait("tensor", fq["w2"], 2 * (j + 1))
        x1banks = ("b0", "b1") if j % 2 == 0 else ("b2", "b3")
        for tc_ in range(2):
            bn = x1banks[tc_]
            BK[bn].acquire()
            for kc in range(16):
                wb = wqbufs["w1"][j % 2][kc // 8]
                off = (kc % 8) * 128
                o = R.op("tensor", "matmul", BK[bn].t[:, 0:512],
                         wb[:, off:off + 128],
                         AR_A[:, kc * NT + tc_ * 512: kc * NT + tc_ * 512 + 512],
                         start=(kc == 0), stop=(kc == 15))
            o.inc(pe, 1)
            if tc_ == 1:
                mm_pe_after[(j, "w1")] = pe.v
            bn2 = ("b4", "b5")[tc_] if j % 2 == 0 else ("b6", "b7")[tc_]
            BK[bn2].acquire()
            for kc in range(16):
                wb = wqbufs["w2"][j % 2][kc // 8]
                off = (kc % 8) * 128
                o = R.op("tensor", "matmul", BK[bn2].t[:, 0:512],
                         wb[:, off:off + 128],
                         AR_A[:, kc * NT + tc_ * 512: kc * NT + tc_ * 512 + 512],
                         start=(kc == 0), stop=(kc == 15))
            o.inc(pe, 1)
            if tc_ == 1:
                mm_pe_after[(j, "w2")] = pe.v
            sg = sgb[(j % 2) * 2 + tc_]
            R.wait("scalar", pe)
            if j >= 2:
                R.wait("scalar", hid, 2 * (j - 2) + tc_ + 1)
            R.op("scalar", "activation", out=sg[:], in_=BK[bn].t[:, 0:512],
                 func=AF.Sigmoid).inc(sgd, 1)
            BK[bn].release("scalar")
            R.wait("vector", sgd)
            R.op("vector", "tensor_tensor", out=hidden_ap(j, tc_ * 512, 512),
                 in0=BK[bn2].t[:, 0:512], in1=sg[:], op=ALU.mult)\
                .inc(hid, 1)
            BK[bn2].release("vector")

    w3f = [SCRF[:, 0:1408], SCRF[:, 1408:2816]]
    w3q = [[AR_A[:, par * 5632 + i * 1408: par * 5632 + (i + 1) * 1408]
            for i in range(4)] for par in range(2)]
    t3a = SCRB[:, 4096:5504]
    t3b = SCRB[:, 5504:6912]
    x1b = [SCRB[:, 0:1024], SCRB[:, 1024:2048]]
    f3ld = Stream(R, "f3ld", 2)
    f3q = R.sem("f3q")
    x1ld = Stream(R, "x1ld", 2)
    oev = R.sem("oev")
    ost = Stream(R, "ost", 2)
    R.wait("sync", fq["w1"], 88)
    R.wait("sync", fq["w2"], 88)
    R.wait("vector", sgd, 88)
    R.wait("vector", hid, 88)
    R.wait("gpsimd", pe)
    nld3 = 0
    qtok3 = {}
    w3_pe_after = {}

    def emit_quant8(jq):
        p = jq % 2
        if jq >= 2:
            R.wait("sync", oev, (jq - 1) * 2)
        x1ld.dma("sync", p, x1b[p][:], x1_spill[:, jq * NT:(jq + 1) * NT])
        for part in range(4):
            g = jq * 4 + part
            if g >= 2 and (g - 2) in qtok3:
                R.wait("sync", qbar["vector"], qtok3[g - 2])
            f3ld.dma("sync", part % 2, w3f[part % 2][:],
                     w3t[:, jq * 5632 + part * 1408: jq * 5632 + (part + 1) * 1408])
            f3ld.wait("vector", part % 2)
            R.wait("vector", dve, th_ready)
            col = thcol["w3"]
            if g >= 1:
                R.wait("vector", f3q, g)
            R.op("vector", "tensor_scalar", out=t3a[:], in0=w3f[part % 2][:],
                 scalar1=S["thb"][:, col:col + 1], scalar2=None, op0=ALU.is_ge)\
                .inc(qbar["vector"], 1)
            R.op("vector", "tensor_scalar", out=t3b[:], in0=w3f[part % 2][:],
                 scalar1=S["nthb"][:, col:col + 1], scalar2=None, op0=ALU.is_le)\
                .inc(qbar["vector"], 1)
            qbar_n["vector"] += 2
            qtok3[g] = qbar_n["vector"]
            R.wait("gpsimd", qbar["vector"], qbar_n["vector"])
            if jq >= 2 and (jq - 2) in w3_pe_after:
                R.wait("gpsimd", pe, w3_pe_after[jq - 2])
            R.op("gpsimd", "tensor_tensor", out=w3q[jq % 2][part][:], in0=t3a[:],
                 in1=t3b[:], op=ALU.subtract).inc(f3q, 1)

    emit_quant8(0)
    for j in range(16):
        p = j % 2
        if j + 1 < 16:
            emit_quant8(j + 1)
        R.wait("tensor", f3q, (j + 1) * 4)
        if j == 0:
            R.wait("tensor", hid, 88)
        pn3 = ("b0", "b1") if j % 2 == 0 else ("b2", "b3")
        for tc_ in range(2):
            bn = pn3[tc_]
            BK[bn].acquire()
            for kc in range(44):
                part = kc // 11
                off = (kc % 11) * 128
                o = R.op("tensor", "matmul", BK[bn].t[:, 0:512],
                         w3q[j % 2][part][:, off:off + 128],
                         hidden_ap(kc, tc_ * 512, 512),
                         start=(kc == 0), stop=(kc == 43))
            o.inc(pe, 1)
            R.wait("vector", pe)
            x1ld.wait("vector", p)
            if j >= 1:
                ost.wait("vector", tc_, j)
            R.op("vector", "tensor_tensor", out=OSTG[tc_][:],
                 in0=BK[bn].t[:, 0:512],
                 in1=x1b[p][:, tc_ * 512: tc_ * 512 + 512],
                 op=ALU.add).inc(oev, 1)
            BK[bn].release("vector")
            R.wait("scalar", oev)
            ost.dma("scalar", tc_,
                    out_c[j * 128:(j + 1) * 128, tc_ * 512:(tc_ + 1) * 512],
                    OSTG[tc_][:])
        w3_pe_after[j] = pe.v
    ost.wait_all("scalar")
    R.replay()
    return nc



_CACHE = {}


def _tile_wT(w, oc):
    t = w.reshape(oc, 128, 16, 128).transpose(3, 0, 2, 1)
    return np.ascontiguousarray(t.reshape(128, oc * 16 * 128))


def _tile_w3T(w3):
    t = w3.reshape(16, 128, 44, 128).transpose(3, 0, 2, 1)
    return np.ascontiguousarray(t.reshape(128, 16 * 44 * 128))


def kernel(L0_attn_x, L0_attn_slope_rate, L0_attn_W_qkvu, L0_attn_W_out,
           L0_ffn_w1, L0_ffn_w2, L0_ffn_w3):
    x = np.asarray(L0_attn_x, dtype=np.float32).reshape(8192, DM)
    slope = np.ascontiguousarray(
        np.asarray(L0_attn_slope_rate, dtype=np.float32).reshape(1, NH))
    Wq = np.asarray(L0_attn_W_qkvu, dtype=np.float32)
    Wo = np.asarray(L0_attn_W_out, dtype=np.float32)
    w1 = np.asarray(L0_ffn_w1, dtype=np.float32)
    w2 = np.asarray(L0_ffn_w2, dtype=np.float32)
    w3 = np.asarray(L0_ffn_w3, dtype=np.float32)

    xT = np.ascontiguousarray(x.T)
    wqkvu_t = _tile_wT(Wq, 64)
    wout_t = _tile_wT(Wo, 16)
    w1_t = _tile_wT(w1, 44)
    w2_t = _tile_wT(w2, 44)
    w3_t = _tile_w3T(w3)

    consts = make_consts()
    if "A" not in _CACHE:
        _CACHE["A"] = build_prog_A(consts)
        _CACHE["B"] = build_prog_B(consts)
    ncA, ncB = _CACHE["A"], _CACHE["B"]

    in_A = []
    for c in range(8):
        in_A.append({
            "xT": np.ascontiguousarray(xT[:, c * NT:(c + 1) * NT]),
            "slope": slope,
            "wqkvu": wqkvu_t,
            "w1s": np.ascontiguousarray(w1_t[:, c * 11264:(c + 1) * 11264]),
            "w2s": np.ascontiguousarray(w2_t[:, c * 11264:(c + 1) * 11264]),
            "w3s": np.ascontiguousarray(w3_t[:, c * 11264:(c + 1) * 11264]),
        })
    resA = run_bass_kernel_spmd(ncA, in_A, core_ids=list(range(8)))

    kv_all = np.ascontiguousarray(
        np.concatenate([resA.results[c]["kv_out"] for c in range(8)], axis=0))
    gparts = np.ascontiguousarray(
        np.concatenate([resA.results[c]["gpart"] for c in range(8)], axis=0))

    in_B = []
    for c in range(8):
        cv = np.full((8, 1), -1e9, dtype=np.float32)
        for cp in range(c):
            cv[cp, 0] = -1024.0 * (c - 1 - cp)
        in_B.append({
            "xT": in_A[c]["xT"],
            "slope": slope,
            "cvec": cv,
            "kv_all": kv_all,
            "gparts": gparts,
            "qs_in": np.asarray(resA.results[c]["qs_out"]),
            "o_in": np.asarray(resA.results[c]["o_out"]),
            "u_in": np.asarray(resA.results[c]["u_out"]),
            "wout": wout_t,
            "w1t": w1_t,
            "w2t": w2_t,
            "w3t": w3_t,
        })
    resB = run_bass_kernel_spmd(ncB, in_B, core_ids=list(range(8)))

    out = np.concatenate([resB.results[c]["out_c"] for c in range(8)], axis=1)
    return np.ascontiguousarray(out.T).reshape(1, 8192, DM).astype(np.float32)
